# version 02
# baseline (speedup 1.0000x reference)
import numpy as np
import ml_dtypes
from contextlib import ExitStack

import concourse.bass as bass
import concourse.mybir as mybir
from concourse.bass_utils import run_bass_kernel_spmd

F32 = mybir.dt.float32
BF16 = mybir.dt.bfloat16
I32 = mybir.dt.int32
AF = mybir.ActivationFunctionType
ALU = mybir.AluOpType

T = 2048
TB = 512
NBLK = 4
KC = 32
IC = 32
HC = 86
NDG = 16
NQH = 32
NKH = 8
INVSQ = 0.08838834764831845
RECD = 1.0 / 4096.0
EPS = 1e-5
GROUPS = [[0, 1], [2, 3], [4, 5], [6, 7]]


class Sem:
    __slots__ = ("h", "n")

    def __init__(self, ctx, nc, name):
        self.h = ctx.enter_context(nc.semaphore(name))
        self.n = 0

    def inc(self, instr, k=1):
        instr.then_inc(self.h, k)
        self.n += k
        return (self, self.n)

    def mark(self):
        return (self, self.n)


def w8(eng, mark):
    if mark is not None:
        eng.wait_ge(mark[0].h, mark[1])


class Stream:

    def __init__(self, ctx, nc, name, shape, dtype):
        self.bufs = [ctx.enter_context(nc.sbuf_tensor(f"{name}{i}", shape, dtype))
                     for i in range(2)]
        self.sems = [Sem(ctx, nc, f"s_{name}{i}") for i in range(2)]
        self.gates = [[], []]
        self.marks = [None, None]
        self.idx = 0

    def load(self, q, in_ap, out_slicer=None, extra_gates=()):
        p = self.idx % 2
        for g in self.gates[p]:
            w8(q, g)
        for g in extra_gates:
            w8(q, g)
        self.gates[p] = []
        out = self.bufs[p][:, :] if out_slicer is None else out_slicer(self.bufs[p])
        self.marks[p] = self.sems[p].inc(q.dma_start(out=out, in_=in_ap), 16)
        self.idx += 1
        return p, self.marks[p]

    def ready(self, p):
        return self.marks[p]

    def consumed(self, p, mark):
        self.gates[p].append(mark)


class WStream(Stream):

    def __init__(self, ctx, nc, name, shape, dtype):
        super().__init__(ctx, nc, name, shape, dtype)
        self.wsems = [Sem(ctx, nc, f"s_{name}w{i}") for i in range(2)]

    def scratch_write(self, q, p, ld_mark, out_ap, src_slicer=None):
        w8(q, ld_mark)
        src = self.bufs[p][:, :] if src_slicer is None else src_slicer(self.bufs[p])
        self.wsems[p].inc(q.dma_start(out=out_ap, in_=src), 16)

    def wmarks(self):
        return [self.wsems[0].mark(), self.wsems[1].mark()]


def build_nc(groups=None, debug=False):
    assert NBLK == T // TB and KC == 2 * T // 128
    groups = groups or GROUPS
    nc = bass.Bass("TRN2", target_bir_lowering=False)
    dbg = {}
    if debug:
        dbg["x1"] = nc.dram_tensor("o_x1", [4096, T], F32, kind="ExternalOutput").ap()
        dbg["x2"] = nc.dram_tensor("o_x2", [4096, T], F32, kind="ExternalOutput").ap()
        dbg["kf"] = nc.dram_tensor("o_kf", [2048, T], BF16, kind="ExternalOutput").ap()
        dbg["vf"] = nc.dram_tensor("o_vf", [2 * T, 1024], BF16, kind="ExternalOutput").ap()
        dbg["q0"] = nc.dram_tensor("o_q0", [128, T], BF16, kind="ExternalOutput").ap()
        dbg["mq2"] = nc.dram_tensor("o_mq2", [1, 32], F32, kind="ExternalOutput").ap()
        dbg["mk2"] = nc.dram_tensor("o_mk2", [1, 32], F32, kind="ExternalOutput").ap()
        dbg["ubc"] = nc.dram_tensor("o_ubc", [128, 32], F32, kind="ExternalOutput").ap()

    x_in = nc.dram_tensor("x_fm", [4096, T], F32, kind="ExternalInput").ap()
    wg4 = nc.dram_tensor("wg4", [HC, 128, IC, 128], I32, kind="ExternalInput").ap()
    wu4 = nc.dram_tensor("wu4", [HC, 128, IC, 128], I32, kind="ExternalInput").ap()
    wd4 = nc.dram_tensor("wd4", [NDG, HC, 128, 256], I32, kind="ExternalInput").ap()
    wqk4 = nc.dram_tensor("wqk4", [40, 128, IC, 128], F32, kind="ExternalInput").ap()
    wvr = nc.dram_tensor("wvr", [IC, 128, 1024], F32, kind="ExternalInput").ap()
    bqk = nc.dram_tensor("bqk", [128, 40], F32, kind="ExternalInput").ap()
    bvb = nc.dram_tensor("bvb", [128, 1024], F32, kind="ExternalInput").ap()
    scg = nc.dram_tensor("scg", [128, 1], F32, kind="ExternalInput").ap()
    scu = nc.dram_tensor("scu", [128, 1], F32, kind="ExternalInput").ap()
    scd = nc.dram_tensor("scd", [128, 1], F32, kind="ExternalInput").ap()
    cosd = nc.dram_tensor("cosd", [128, T], BF16, kind="ExternalInput").ap()
    sind = nc.dram_tensor("sind", [128, T], BF16, kind="ExternalInput").ap()
    rmtd = nc.dram_tensor("rmtd", [128, 128], BF16, kind="ExternalInput").ap()
    onesd = nc.dram_tensor("onesd", [128, 129], BF16, kind="ExternalInput").ap()
    ones1d = nc.dram_tensor("ones1d", [1, 128], F32, kind="ExternalInput").ap()

    y_out = nc.dram_tensor("y_fm", [4096, T], F32, kind="ExternalOutput").ap()

    wg16 = nc.dram_tensor("wg16", [HC, 128, IC * 128], BF16, kind="Internal").ap()
    wu16 = nc.dram_tensor("wu16", [HC, 128, IC * 128], BF16, kind="Internal").ap()
    wd16 = nc.dram_tensor("wd16", [NDG, HC, 128, 256], BF16, kind="Internal").ap()
    wqk16 = nc.dram_tensor("wqk16", [40, 128, IC * 128], BF16, kind="Internal").ap()
    wv16 = nc.dram_tensor("wv16", [IC, 128, 1024], BF16, kind="Internal").ap()
    x1_d = nc.dram_tensor("x1_d", [4096, T], F32, kind="Internal").ap()
    x2_d = nc.dram_tensor("x2_d", [4096, T], F32, kind="Internal").ap()
    q_d = nc.dram_tensor("q_d", [NQH, 128, T], BF16, kind="Internal").ap()
    k_loc = nc.dram_tensor("k_loc", [1024, T], BF16, kind="Internal").ap()
    v_loc = nc.dram_tensor("v_loc", [T, 1024], BF16, kind="Internal").ap()
    k_full = nc.dram_tensor("k_full", [2048, T], BF16, kind="Internal").ap()
    v_full = nc.dram_tensor("v_full", [2 * T, 1024], BF16, kind="Internal").ap()
    mk2_d = nc.dram_tensor("mk2_d", [1, 32], F32, kind="Internal").ap()
    mk2r_d = nc.dram_tensor("mk2r_d", [1, 32], F32, kind="Internal").ap()

    with ExitStack() as ctx:
        P = [ctx.enter_context(nc.psum_tensor(f"P{i}", [128, TB], F32))
             for i in range(8)]

        cosb = ctx.enter_context(nc.sbuf_tensor("cosb", [128, T], BF16))
        sinb = ctx.enter_context(nc.sbuf_tensor("sinb", [128, T], BF16))
        rmt = ctx.enter_context(nc.sbuf_tensor("rmt", [128, 128], BF16))
        onesm = ctx.enter_context(nc.sbuf_tensor("onesm", [128, 129], BF16))
        ones1 = ctx.enter_context(nc.sbuf_tensor("ones1", [1, 128], F32))
        bqk_sb = ctx.enter_context(nc.sbuf_tensor("bqk_sb", [128, 40], F32))
        bvb_sb = ctx.enter_context(nc.sbuf_tensor("bvb_sb", [128, 1024], F32))
        sc_g = ctx.enter_context(nc.sbuf_tensor("sc_g", [128, 1], F32))
        sc_u = ctx.enter_context(nc.sbuf_tensor("sc_u", [128, 1], F32))
        sc_d = ctx.enter_context(nc.sbuf_tensor("sc_d", [128, 1], F32))
        mq2 = ctx.enter_context(nc.sbuf_tensor("mq2", [1, 32], F32))
        mk2 = ctx.enter_context(nc.sbuf_tensor("mk2", [1, 32], F32))
        mq2l = ctx.enter_context(nc.sbuf_tensor("mq2l", [1, 32], F32))
        rtmp = ctx.enter_context(nc.sbuf_tensor("rtmp", [1, 1], F32))
        rowm = ctx.enter_context(nc.sbuf_tensor("rowm", [1, TB], F32))
        rowa = ctx.enter_context(nc.sbuf_tensor("rowa", [1, TB], F32))
        rowb = ctx.enter_context(nc.sbuf_tensor("rowb", [1, TB], F32))
        ubc = ctx.enter_context(nc.sbuf_tensor("ubc", [128, 32], F32))

        sLd = Sem(ctx, nc, "sLd")
        sMM = Sem(ctx, nc, "sMM")
        sSil = Sem(ctx, nc, "sSil")
        sEvGU = Sem(ctx, nc, "sEvGU")
        sMMd = Sem(ctx, nc, "sMMd")
        sEvD = Sem(ctx, nc, "sEvD")
        sLn2P = Sem(ctx, nc, "sLn2P")
        sLnP = Sem(ctx, nc, "sLnP")
        sLnM = Sem(ctx, nc, "sLnM")
        sFin = Sem(ctx, nc, "sFin")
        sNorm = Sem(ctx, nc, "sNorm")
        sQmm = Sem(ctx, nc, "sQmm")
        sQact = Sem(ctx, nc, "sQact")
        sRot = Sem(ctx, nc, "sRot")
        sQdve = Sem(ctx, nc, "sQdve")
        sQnm = Sem(ctx, nc, "sQnm")
        sQnr = Sem(ctx, nc, "sQnr")
        sVmm = Sem(ctx, nc, "sVmm")
        sVev = Sem(ctx, nc, "sVev")
        sQK = Sem(ctx, nc, "sQK")
        sExp = Sem(ctx, nc, "sExp")
        sPV = Sem(ctx, nc, "sPV")
        sAev = Sem(ctx, nc, "sAev")
        sAevW = Sem(ctx, nc, "sAevW")
        sCC = Sem(ctx, nc, "sCC")
        sKV = Sem(ctx, nc, "sKV")
        sBar = Sem(ctx, nc, "sBar")
        sMk = Sem(ctx, nc, "sMk")
        sX1w = [Sem(ctx, nc, f"sX1w{i}") for i in range(2)]
        sQw = [Sem(ctx, nc, f"sQw{i}") for i in range(2)]
        sKw = [Sem(ctx, nc, f"sKw{i}") for i in range(2)]
        sVw = [Sem(ctx, nc, f"sVw{i}") for i in range(2)]
        sX2w = [Sem(ctx, nc, f"sX2w{i}") for i in range(2)]
        sYw = [Sem(ctx, nc, f"sYw{i}") for i in range(2)]

        blk = ctx.enter_context(nc.Block())

        pgate = [None] * 8

        def bank_wait(eng, i):
            w8(eng, pgate[i])

        state = {}
        env = {}

        def prelude():
            @blk.sync
            def _(q):
                for dst, src in ((cosb, cosd), (sinb, sind), (rmt, rmtd),
                                 (onesm, onesd), (ones1, ones1d),
                                 (bqk_sb, bqk), (bvb_sb, bvb),
                                 (sc_g, scg), (sc_u, scu), (sc_d, scd)):
                    sLd.inc(q.dma_start(out=dst[:, :], in_=src), 16)
            ld_all = sLd.mark()

            @blk.vector
            def _(v):
                w8(v, ld_all)
                v.tensor_scalar_mul(sc_d[:, :], sc_d[:, :], 0.5)
                v.memset(mq2[:, :], 0.0)
                m = v.memset(mk2[:, :], 0.0)
                sFin.inc(m)

        def emit_ln(B, src_d):
            xstr, ebuf, fbuf = env["xstr"], env["ebuf"], env["fbuf"]
            u_sb, ntmp = env["u_sb"], env["ntmp"]
            c0, c1 = B * TB, (B + 1) * TB

            s0 = xstr.idx
            base_lnp = sLnP.n
            base_lnm = sLnM.n

            mX = []

            @blk.gpsimd
            def _(q):
                for icp in range(16):
                    ap = src_d[icp * 256:(icp + 1) * 256, c0:c1] \
                        .rearrange("(c p) t -> p c t", c=2)
                    extra = [(sLnP, base_lnp + icp - 1)] if icp >= 2 else []
                    mX.append(xstr.load(
                        q, ap, out_slicer=lambda b: b[:, :].rearrange(
                            "p (c t) -> p c t", c=2), extra_gates=extra)[1])
            eg = state.get("ebuf_gate") or []

            @blk.vector
            def _(v):
                for icp in range(16):
                    p = (s0 + icp) % 2
                    w8(v, mX[icp])
                    if icp >= 2:
                        v.wait_ge(sLnM.h, base_lnm + icp - 1)
                    else:
                        v.wait_ge(sLnM.h, base_lnm)
                        for g in eg:
                            w8(v, g)
                    v.tensor_copy(ebuf[p][:, :], xstr.bufs[p][:, :])
                    m = v.tensor_mul(fbuf[p][:, :], ebuf[p][:, :], ebuf[p][:, :])
                    mk = sLnP.inc(m)
                    xstr.consumed(p, mk)
            state["ebuf_gate"] = None

            @blk.tensor
            def _(t):
                for icp in range(16):
                    p = (s0 + icp) % 2
                    t.wait_ge(sLnP.h, base_lnp + icp + 1)
                    if icp == 0:
                        bank_wait(t, 6)
                        bank_wait(t, 7)
                    for half in range(2):
                        st = icp == 0 and half == 0
                        sp = icp == 15 and half == 1
                        t.matmul(P[6][0:1, :], onesm[:, 128:129],
                                 ebuf[p][:, half * TB:(half + 1) * TB],
                                 start=st, stop=sp)
                        m = t.matmul(P[7][0:1, :], onesm[:, 128:129],
                                     fbuf[p][:, half * TB:(half + 1) * TB],
                                     start=st, stop=sp)
                    sLnM.inc(m)
            stats_done = sLnM.mark()

            @blk.vector
            def _(v):
                w8(v, stats_done)
                v.tensor_scalar_mul(rowm[:, :], P[6][0:1, :], RECD)
                v.tensor_mul(rowa[:, :], rowm[:, :], rowm[:, :])
                v.scalar_tensor_tensor(rowb[:, :], P[7][0:1, :], RECD,
                                       rowa[:, :], ALU.mult, ALU.subtract)
                v.tensor_scalar_add(rowb[:, :], rowb[:, :], EPS)
                m = v.tensor_scalar_mul(rowm[:, :], rowm[:, :], -1.0)
                state["fin_var"] = sFin.inc(m)
            pgate[6] = state["fin_var"]
            pgate[7] = state["fin_var"]

            @blk.scalar
            def _(s):
                w8(s, state["fin_var"])
                s.activation(rowa[:, :], rowb[:, :], AF.Ln)
                m = s.activation(rowb[:, :], rowa[:, :], AF.Exp, scale=-0.5)
                state["fin_rstd"] = sFin.inc(m)

            @blk.tensor
            def _(t):
                w8(t, state["fin_rstd"])
                bank_wait(t, 2)
                bank_wait(t, 3)
                t.matmul(P[2][:, :], ones1[:, :], rowm[:, :], start=True, stop=True)
                m = t.matmul(P[3][:, :], ones1[:, :], rowb[:, :], start=True, stop=True)
                state["fin_bc"] = sLnM.inc(m)

            n0 = xstr.idx
            base_nrm = sNorm.n

            mN = []

            @blk.gpsimd
            def _(q):
                for icp in range(16):
                    ap = src_d[icp * 256:(icp + 1) * 256, c0:c1] \
                        .rearrange("(c p) t -> p c t", c=2)
                    extra = [(sNorm, base_nrm + icp - 1)] if icp >= 2 else []
                    mN.append(xstr.load(q, ap,
                              out_slicer=lambda b: b[:, :].rearrange(
                                  "p (c t) -> p c t", c=2),
                              extra_gates=extra)[1])

            @blk.vector
            def _(v):
                w8(v, state["fin_bc"])
                for icn in range(16):
                    p = (n0 + icn) % 2
                    w8(v, mN[icn])
                    for half in range(2):
                        ic = icn * 2 + half
                        v.tensor_add(ntmp[:, :],
                                     xstr.bufs[p][:, half * TB:(half + 1) * TB],
                                     P[2][:, :])
                        m = v.tensor_mul(u_sb[:, ic * TB:(ic + 1) * TB],
                                         ntmp[:, :], P[3][:, :])
                    mk = sNorm.inc(m)
                    xstr.consumed(p, mk)
            state["norm_done"] = sNorm.mark()
            pgate[2] = state["norm_done"]
            pgate[3] = state["norm_done"]

        def emit_gu(B, conv, first_scr_gates=()):
            xstr, gbuf = env["xstr"], env["gbuf"]
            u_sb, mid_sb = env["u_sb"], env["mid_sb"]
            wbA, wbB = env["wbA"], env["wbB"]
            base_mm = sMM.n
            base_sil = sSil.n
            base_ev = sEvGU.n
            l0 = wbA.idx
            baseWA = [wbA.wsems[0].n, wbA.wsems[1].n] if conv else None
            baseWB = [wbB.wsems[0].n, wbB.wsems[1].n] if conv else None
            mA, mB = [], []

            def loads(q):
                for g in range(HC):
                    extra = list(first_scr_gates) if g == 0 else []
                    if g >= 2:
                        extra.append((sMM, base_mm + g - 1))
                        if conv:
                            p = (l0 + g) % 2
                            nw = 16 * ((g - 2) // 2 + 1)
                            extra.append((wbA.wsems[p], baseWA[p] + nw))
                            extra.append((wbB.wsems[p], baseWB[p] + nw))
                    if conv:
                        mA.append(wbA.load(
                            q, wg4[g].rearrange("p ic h -> p (ic h)"),
                            extra_gates=extra)[1])
                        mB.append(wbB.load(
                            q, wu4[g].rearrange("p ic h -> p (ic h)"),
                            extra_gates=extra)[1])
                    else:
                        mA.append(wbA.load(q, wg16[g], extra_gates=extra)[1])
                        mB.append(wbB.load(q, wu16[g], extra_gates=extra)[1])
            getattr(blk, "gpsimd" if conv else "sync")(loads)

            if conv:
                @blk.sync
                def _(q):
                    for g in range(HC):
                        wbA.scratch_write(q, (l0 + g) % 2, mA[g], wg16[g])
                        wbB.scratch_write(q, (l0 + g) % 2, mB[g], wu16[g])

            @blk.tensor
            def _(t):
                w8(t, state["norm_done"])
                for g in range(HC):
                    p = (l0 + g) % 2
                    w8(t, mA[g])
                    w8(t, mB[g])
                    if g >= 2:
                        t.wait_ge(sSil.h, base_sil + g - 1)
                        t.wait_ge(sEvGU.h, base_ev + g - 1)
                    elif g == 0:
                        for i in range(4):
                            bank_wait(t, i)
                    for ic in range(IC):
                        st = ic == 0
                        sp = ic == IC - 1
                        t.matmul(P[p][:, :],
                                 wbA.bufs[p][:, ic * 128:(ic + 1) * 128],
                                 u_sb[:, ic * TB:(ic + 1) * TB], start=st, stop=sp)
                        m = t.matmul(P[2 + p][:, :],
                                     wbB.bufs[p][:, ic * 128:(ic + 1) * 128],
                                     u_sb[:, ic * TB:(ic + 1) * TB], start=st, stop=sp)
                    mk = sMM.inc(m)
                    wbA.consumed(p, mk)
                    wbB.consumed(p, mk)

            @blk.scalar
            def _(s):
                for g in range(HC):
                    p = (l0 + g) % 2
                    s.wait_ge(sMM.h, base_mm + g + 1)
                    if g >= 2:
                        s.wait_ge(sEvGU.h, base_ev + g - 1)
                    m = s.activation(gbuf[p][:, 0:TB], P[p][:, :], AF.Silu,
                                     scale=sc_g[:, 0:1])
                    sSil.inc(m)

            @blk.vector
            def _(v):
                for g in range(HC):
                    p = (l0 + g) % 2
                    v.wait_ge(sSil.h, base_sil + g + 1)
                    m = v.scalar_tensor_tensor(
                        mid_sb[:, g * TB:(g + 1) * TB], P[2 + p][:, :],
                        sc_u[:, 0:1], gbuf[p][:, 0:TB], ALU.mult, ALU.mult)
                    sEvGU.inc(m)
            state["gu_done"] = sEvGU.mark()
            for i in range(4):
                pgate[i] = state["gu_done"]

        def emit_down(B, conv, x_src, out_d, out_sems, ride_ln2,
                      first_scr_gates=()):
            xstr, ebuf, fbuf = env["xstr"], env["ebuf"], env["fbuf"]
            mid_sb, wbC = env["mid_sb"], env["wbC"]
            c0, c1 = B * TB, (B + 1) * TB
            runs = []
            for dg in range(NDG):
                for h0 in range(0, HC, 8):
                    runs.append((dg, h0, min(h0 + 8, HC) - h0))
            rpd = (HC + 7) // 8

            base_mmd = sMMd.n
            base_evd = sEvD.n
            base_l2p = sLn2P.n
            base_l2m = sLnM.n
            gu_base = state["gu_base_for_down"]
            l0 = wbC.idx
            x0 = xstr.idx
            baseWC = [wbC.wsems[0].n, wbC.wsems[1].n] if conv else None
            baseOut = [out_sems[0].n, out_sems[1].n]
            mC = []

            def wload(q, R):
                dg, h0, n = runs[R]
                extra = list(first_scr_gates) if R == 0 else []
                if R >= 2:
                    extra.append((sMMd, base_mmd + R - 1))
                    if conv:
                        p = (l0 + R) % 2
                        extra.append((wbC.wsems[p],
                                      baseWC[p] + 16 * ((R - 2) // 2 + 1)))
                src = (wd4 if conv else wd16)[dg, h0:h0 + n] \
                    .rearrange("hc p d -> p hc d")
                mC.append(wbC.load(
                    q, src,
                    out_slicer=lambda b, n=n: b[:, 0:n * 256].rearrange(
                        "p (hc d) -> p hc d", hc=n),
                    extra_gates=extra)[1])

            mXd = []

            def xload(q, dg):
                ap = x_src[dg * 256:(dg + 1) * 256, c0:c1] \
                    .rearrange("(c p) t -> p c t", c=2)
                extra = []
                if dg >= 2:
                    extra.append((sEvD, base_evd + dg - 1))
                    if ride_ln2:
                        extra.append((sLn2P, base_l2p + dg - 1))
                    p2 = dg % 2
                    extra.append((out_sems[p2],
                                  baseOut[p2] + 16 * ((dg - 2) // 2 + 1)))
                mXd.append(xstr.load(q, ap,
                           out_slicer=lambda b: b[:, :].rearrange(
                               "p (c t) -> p c t", c=2),
                           extra_gates=extra)[1])

            if conv:
                @blk.gpsimd
                def _(q):
                    for R in range(len(runs)):
                        wload(q, R)
                        if R % rpd == 1:
                            xload(q, R // rpd)
            else:
                @blk.sync
                def _(q):
                    for R in range(len(runs)):
                        wload(q, R)

                @blk.gpsimd
                def _(q):
                    for dg in range(NDG):
                        xload(q, dg)

            if conv:
                @blk.sync
                def _(q):
                    for R, (dg, h0, n) in enumerate(runs):
                        wbC.scratch_write(
                            q, (l0 + R) % 2, mC[R],
                            wd16[dg, h0:h0 + n].rearrange("hc p d -> p hc d"),
                            src_slicer=lambda b, n=n: b[:, 0:n * 256].rearrange(
                                "p (hc d) -> p hc d", hc=n))

            def _stats_mm(t, dg):
                p = (x0 + dg) % 2
                t.wait_ge(sLn2P.h, base_l2p + dg + 1)
                if dg == 0:
                    bank_wait(t, 0)
                    bank_wait(t, 1)
                for half in range(2):
                    st = dg == 0 and half == 0
                    sp = dg == NDG - 1 and half == 1
                    t.matmul(P[0][0:1, :], onesm[:, 128:129],
                             ebuf[p][:, half * TB:(half + 1) * TB],
                             start=st, stop=sp)
                    m = t.matmul(P[1][0:1, :], onesm[:, 128:129],
                                 fbuf[p][:, half * TB:(half + 1) * TB],
                                 start=st, stop=sp)
                sLnM.inc(m)

            @blk.tensor
            def _(t):
                for R, (dg, h0, n) in enumerate(runs):
                    p = (l0 + R) % 2
                    qq = dg % 2
                    w8(t, mC[R])
                    if h0 == 0:
                        if ride_ln2 and dg >= 2:
                            _stats_mm(t, dg - 2)
                        if dg >= 2:
                            t.wait_ge(sEvD.h, base_evd + dg - 1)
                        else:
                            bank_wait(t, 4 + qq * 2)
                            bank_wait(t, 5 + qq * 2)
                    t.wait_ge(sEvGU.h, gu_base + h0 + n)
                    for j in range(n):
                        hc = h0 + j
                        for dc in range(2):
                            m = t.matmul(
                                P[4 + qq * 2 + dc][:, :],
                                wbC.bufs[p][:, j * 256 + dc * 128:
                                            j * 256 + (dc + 1) * 128],
                                mid_sb[:, hc * TB:(hc + 1) * TB],
                                start=(hc == 0), stop=(hc == HC - 1))
                    mk = sMMd.inc(m)
                    wbC.consumed(p, mk)
                if ride_ln2:
                    _stats_mm(t, NDG - 2)
                    _stats_mm(t, NDG - 1)

            @blk.vector
            def _(v):
                for dg in range(NDG):
                    qq = dg % 2
                    p = (x0 + dg) % 2
                    v.wait_ge(sMMd.h, base_mmd + (dg + 1) * rpd)
                    w8(v, mXd[dg])
                    for dc in range(2):
                        m = v.scalar_tensor_tensor(
                            xstr.bufs[p][:, dc * TB:(dc + 1) * TB],
                            P[4 + qq * 2 + dc][:, :], sc_d[:, 0:1],
                            xstr.bufs[p][:, dc * TB:(dc + 1) * TB],
                            ALU.mult, ALU.add)
                    sEvD.inc(m)
                    if ride_ln2:
                        if dg >= 2:
                            v.wait_ge(sLnM.h, base_l2m + dg - 1)
                        else:
                            v.wait_ge(sLnM.h, base_l2m)
                        v.tensor_copy(ebuf[p][:, :], xstr.bufs[p][:, :])
                        m2 = v.tensor_mul(fbuf[p][:, :], ebuf[p][:, :],
                                          ebuf[p][:, :])
                        sLn2P.inc(m2)
            state["down_ev"] = sEvD.mark()
            for i in range(4, 8):
                pgate[i] = state["down_ev"]

            if ride_ln2:
                state["ln2_stats"] = sLnM.mark()

            @blk.scalar
            def _(s):
                for dg in range(NDG):
                    p = (x0 + dg) % 2
                    s.wait_ge(sEvD.h, base_evd + dg + 1)
                    out_ap = out_d[dg * 256:(dg + 1) * 256, c0:c1] \
                        .rearrange("(c p) t -> p c t", c=2)
                    m = out_sems[dg % 2].inc(
                        s.dma_start(out=out_ap,
                                    in_=xstr.bufs[p][:, :].rearrange(
                                        "p (c t) -> p c t", c=2)), 16)
                    xstr.consumed(p, m)
                    if ride_ln2:
                        xstr.consumed(p, (sLn2P, base_l2p + dg + 1))

        def emit_ln2_fin(B, src_d):
            xstr, u_sb, ntmp = env["xstr"], env["u_sb"], env["ntmp"]
            c0, c1 = B * TB, (B + 1) * TB

            @blk.vector
            def _(v):
                w8(v, state["ln2_stats"])
                v.tensor_scalar_mul(rowm[:, :], P[0][0:1, :], RECD)
                v.tensor_mul(rowa[:, :], rowm[:, :], rowm[:, :])
                v.scalar_tensor_tensor(rowb[:, :], P[1][0:1, :], RECD,
                                       rowa[:, :], ALU.mult, ALU.subtract)
                v.tensor_scalar_add(rowb[:, :], rowb[:, :], EPS)
                m = v.tensor_scalar_mul(rowm[:, :], rowm[:, :], -1.0)
                state["fin2_var"] = sFin.inc(m)
            pgate[0] = state["fin2_var"]
            pgate[1] = state["fin2_var"]

            @blk.scalar
            def _(s):
                w8(s, state["fin2_var"])
                s.activation(rowa[:, :], rowb[:, :], AF.Ln)
                m = s.activation(rowb[:, :], rowa[:, :], AF.Exp, scale=-0.5)
                state["fin2_rstd"] = sFin.inc(m)

            @blk.tensor
            def _(t):
                w8(t, state["fin2_rstd"])
                bank_wait(t, 2)
                bank_wait(t, 3)
                t.matmul(P[2][:, :], ones1[:, :], rowm[:, :], start=True, stop=True)
                m = t.matmul(P[3][:, :], ones1[:, :], rowb[:, :], start=True, stop=True)
                state["fin2_bc"] = sLnM.inc(m)

            n0 = xstr.idx
            base_nrm2 = sNorm.n
            x1g = [sX1w[0].mark(), sX1w[1].mark()]

            mN2 = []

            @blk.gpsimd
            def _(q):
                for icp in range(16):
                    ap = src_d[icp * 256:(icp + 1) * 256, c0:c1] \
                        .rearrange("(c p) t -> p c t", c=2)
                    extra = list(x1g) if icp < 2 else \
                        [(sNorm, base_nrm2 + icp - 1)]
                    mN2.append(xstr.load(q, ap,
                               out_slicer=lambda b: b[:, :].rearrange(
                                   "p (c t) -> p c t", c=2),
                               extra_gates=extra)[1])

            @blk.vector
            def _(v):
                w8(v, state["fin2_bc"])
                for icn in range(16):
                    p = (n0 + icn) % 2
                    w8(v, mN2[icn])
                    for half in range(2):
                        ic = icn * 2 + half
                        v.tensor_add(ntmp[:, :],
                                     xstr.bufs[p][:, half * TB:(half + 1) * TB],
                                     P[2][:, :])
                        m = v.tensor_mul(u_sb[:, ic * TB:(ic + 1) * TB],
                                         ntmp[:, :], P[3][:, :])
                    mk = sNorm.inc(m)
                    xstr.consumed(p, mk)
            state["norm_done"] = sNorm.mark()
            pgate[2] = state["norm_done"]
            pgate[3] = state["norm_done"]

        def emit_qkv(B, conv):
            ebuf, fbuf, gbuf = env["ebuf"], env["fbuf"], env["gbuf"]
            ntmp = env["ntmp"]
            u_sb, wbA, wbC, vstb = env["u_sb"], env["wbA"], env["wbC"], env["vstb"]
            c0, c1 = B * TB, (B + 1) * TB
            base_qmm = sQmm.n
            base_qact = sQact.n
            base_rot = sRot.n
            base_qdve = sQdve.n
            base_qnm = sQnm.n
            base_qnr = sQnr.n
            base_qw = [sQw[0].n, sQw[1].n]
            base_kw = [sKw[0].n, sKw[1].n]
            l0 = wbA.idx
            baseWQ = [wbA.wsems[0].n, wbA.wsems[1].n] if conv else None
            mQ = []

            def wloads(q):
                for oc in range(40):
                    extra = []
                    if oc >= 2:
                        extra = [(sQmm, base_qmm + oc - 1)]
                        if conv:
                            p = (l0 + oc) % 2
                            extra.append((wbA.wsems[p],
                                          baseWQ[p] + 16 * ((oc - 2) // 2 + 1)))
                    if conv:
                        mQ.append(wbA.load(
                            q, wqk4[oc].rearrange("p ic h -> p (ic h)"),
                            extra_gates=extra)[1])
                    else:
                        mQ.append(wbA.load(q, wqk16[oc],
                                           extra_gates=extra)[1])
            getattr(blk, "gpsimd" if conv else "sync")(wloads)

            if conv:
                @blk.sync
                def _(q):
                    for oc in range(40):
                        wbA.scratch_write(q, (l0 + oc) % 2, mQ[oc], wqk16[oc])

            def _rot(t, oc):
                t.wait_ge(sQact.h, base_qact + oc + 1)
                if oc >= 2:
                    t.wait_ge(sQdve.h, base_qdve + oc - 1)
                else:
                    bank_wait(t, 4 + oc % 2)
                m = t.matmul(P[4 + oc % 2][:, :], rmt[:, :],
                             ebuf[oc % 2][:, 0:TB], start=True, stop=True)
                sRot.inc(m)

            def _qn(t, oc):
                t.wait_ge(sQdve.h, base_qdve + oc + 1)
                if oc >= 1:
                    t.wait_ge(sQnr.h, base_qnr + oc)
                else:
                    bank_wait(t, 6)
                m = t.matmul(P[6][0:1, :], onesm[:, 128:129],
                             fbuf[oc % 2][:, 0:TB], start=True, stop=True)
                sQnm.inc(m)

            @blk.tensor
            def _(t):
                w8(t, state["norm_done"])
                for oc in range(40):
                    p = (l0 + oc) % 2
                    w8(t, mQ[oc])
                    if oc >= 2:
                        t.wait_ge(sQact.h, base_qact + oc - 1)
                    elif oc == 0:
                        for i in range(4):
                            bank_wait(t, i)
                    bank = P[oc % 2] if oc < 32 else P[2 + oc % 2]
                    for ic in range(IC):
                        m = t.matmul(bank[:, :],
                                     wbA.bufs[p][:, ic * 128:(ic + 1) * 128],
                                     u_sb[:, ic * TB:(ic + 1) * TB],
                                     start=(ic == 0), stop=(ic == IC - 1))
                    mk = sQmm.inc(m)
                    wbA.consumed(p, mk)
                    if oc >= 1:
                        _rot(t, oc - 1)
                    if oc >= 2:
                        _qn(t, oc - 2)
                _rot(t, 39)
                _qn(t, 38)
                _qn(t, 39)

            def _spill(s, oc):
                p = oc % 2
                s.wait_ge(sQdve.h, base_qdve + oc + 1)
                if oc < 32:
                    out_ap = q_d[oc, :, c0:c1]
                    sem = sQw[p]
                else:
                    out_ap = k_loc[(oc - 32) * 128:(oc - 31) * 128, c0:c1]
                    sem = sKw[p]
                sem.inc(s.dma_start(out=out_ap, in_=ebuf[p][:, TB:2 * TB]), 16)

            @blk.scalar
            def _(s):
                for oc in range(40):
                    s.wait_ge(sQmm.h, base_qmm + oc + 1)
                    if oc >= 2:
                        s.wait_ge(sRot.h, base_rot + oc - 1)
                        s.wait_ge(sQdve.h, base_qdve + oc - 1)
                    bank = P[oc % 2] if oc < 32 else P[2 + oc % 2]
                    m = s.activation(ebuf[oc % 2][:, 0:TB], bank[:, :],
                                     AF.Identity, bias=bqk_sb[:, oc:oc + 1])
                    sQact.inc(m)
                    if oc >= 2:
                        _spill(s, oc - 2)
                _spill(s, 38)
                _spill(s, 39)

            def _qnr(v, oc):
                v.wait_ge(sQnm.h, base_qnm + oc + 1)
                v.tensor_reduce(rtmp[:, :], P[6][0:1, :], mybir.AxisListType.X,
                                ALU.max)
                if oc < 32:
                    m = v.tensor_max(mq2[0:1, oc:oc + 1], mq2[0:1, oc:oc + 1],
                                     rtmp[:, :])
                else:
                    for j in range(4):
                        c = (oc - 32) * 4 + j
                        m = v.tensor_max(mk2[0:1, c:c + 1], mk2[0:1, c:c + 1],
                                         rtmp[:, :])
                sQnr.inc(m)

            @blk.vector
            def _(v):
                for oc in range(40):
                    p = oc % 2
                    v.wait_ge(sQact.h, base_qact + oc + 1)
                    v.wait_ge(sRot.h, base_rot + oc + 1)
                    if oc >= 2:
                        po = oc - 2
                        if po < 32:
                            v.wait_ge(sQw[po % 2].h,
                                      base_qw[po % 2] + 16 * (po // 2 + 1))
                        else:
                            v.wait_ge(sKw[po % 2].h,
                                      base_kw[po % 2] + 16 * ((po - 32) // 2 + 1))
                    v.tensor_mul(gbuf[p][:, 0:TB], ebuf[p][:, 0:TB],
                                 cosb[:, c0:c1])
                    v.tensor_mul(ntmp[:, :], P[4 + p][:, :],
                                 sinb[:, c0:c1])
                    v.tensor_add(ebuf[p][:, TB:2 * TB], gbuf[p][:, 0:TB],
                                 ntmp[:, :])
                    m = v.tensor_mul(fbuf[p][:, 0:TB], ebuf[p][:, TB:2 * TB],
                                     ebuf[p][:, TB:2 * TB])
                    sQdve.inc(m)
                    if oc >= 1:
                        _qnr(v, oc - 1)
                _qnr(v, 39)

            state["qk_done"] = sQnr.mark()

            base_vmm = sVmm.n
            base_vev = sVev.n
            vl0 = wbC.idx
            baseWV = [wbC.wsems[0].n, wbC.wsems[1].n] if conv else None
            mV = []

            def vloads(q):
                for i in range(64):
                    ic = i % IC
                    extra = []
                    if i >= 2:
                        extra = [(sVmm, base_vmm + i - 1)]
                        if conv and i - 2 < IC:
                            p = (vl0 + i) % 2
                            extra.append((wbC.wsems[p],
                                          baseWV[p] + 16 * ((i - 2) // 2 + 1)))
                    src = wvr[ic] if conv else wv16[ic]
                    mV.append(wbC.load(q, src, out_slicer=lambda b: b[:, 0:1024],
                                       extra_gates=extra)[1])
            getattr(blk, "gpsimd" if conv else "sync")(vloads)

            if conv:
                @blk.sync
                def _(q):
                    for ic in range(IC):
                        wbC.scratch_write(q, (vl0 + ic) % 2, mV[ic], wv16[ic],
                                          src_slicer=lambda b: b[:, 0:1024])

            @blk.tensor
            def _(t):
                for tp in range(2):
                    for ic in range(IC):
                        i = tp * IC + ic
                        p = (vl0 + i) % 2
                        w8(t, mV[i])
                        if i == 0:
                            t.wait_ge(sQdve.h, base_qdve + 40)
                            t.wait_ge(sQnr.h, base_qnr + 40)
                            bank_wait(t, 7)
                        if ic == 0 and tp == 1:
                            t.wait_ge(sVev.h, base_vev + 1)
                        for ttl in range(2):
                            for dvg in range(2):
                                m = t.matmul(
                                    P[4 + ttl * 2 + dvg][:, :],
                                    u_sb[:, ic * TB + (tp * 2 + ttl) * 128:
                                         ic * TB + (tp * 2 + ttl + 1) * 128],
                                    wbC.bufs[p][:, dvg * TB:(dvg + 1) * TB],
                                    start=(ic == 0), stop=(ic == IC - 1))
                        mk = sVmm.inc(m)
                        wbC.consumed(p, mk)

            base_vw = [sVw[0].n, sVw[1].n]

            @blk.vector
            def _(v):
                for tp in range(2):
                    v.wait_ge(sVmm.h, base_vmm + (tp + 1) * IC)
                    v.wait_ge(sVw[0].h, base_vw[0] + 64 * tp)
                    v.wait_ge(sVw[1].h, base_vw[1])
                    for ttl in range(2):
                        for dvg in range(2):
                            m = v.tensor_add(
                                vstb[:, (ttl * 2 + dvg) * TB:
                                     (ttl * 2 + dvg + 1) * TB],
                                P[4 + ttl * 2 + dvg][:, :],
                                bvb_sb[:, dvg * TB:(dvg + 1) * TB])
                    sVev.inc(m)
            state["v_done"] = sVev.mark()
            for i in range(4, 8):
                pgate[i] = state["v_done"]
            pgate[0] = sQact.mark()
            pgate[1] = sQact.mark()

            @blk.scalar
            def _(s):
                for tp in range(2):
                    s.wait_ge(sVev.h, base_vev + tp + 1)
                    for ttl in range(2):
                        ttg = tp * 2 + ttl
                        for dvg in range(2):
                            sVw[tp].inc(s.dma_start(
                                out=v_loc[c0 + ttg * 128: c0 + (ttg + 1) * 128,
                                          dvg * TB:(dvg + 1) * TB],
                                in_=vstb[:, (ttl * 2 + dvg) * TB:
                                         (ttl * 2 + dvg + 1) * TB]), 16)

            state["ebuf_gate"] = [sQdve.mark(), sRot.mark(), sQnm.mark(),
                                  sQw[0].mark(), sQw[1].mark(),
                                  sKw[0].mark(), sKw[1].mark()]

        def barrier(pre=None):
            pre = pre or {}
            for eng in ("sync", "gpsimd", "scalar", "vector", "tensor"):
                def f(e, eng=eng):
                    for g in pre.get(eng, []):
                        w8(e, g)
                    sBar.inc(e.nop())
                getattr(blk, eng)(f)
            target = sBar.n

            def waiter(e):
                e.wait_ge(sBar.h, target)
            for eng in ("sync", "gpsimd", "scalar", "vector", "tensor"):
                getattr(blk, eng)(waiter)

        def emit_attention():
            K_sb, V_sb = env["K_sb"], env["V_sb"]
            qhs, x1s = env["qhs"], env["x1s"]
            pb, rbuf, rbuf2, x2st, mq2k = (env["pb"], env["rbuf"],
                                           env["rbuf2"], env["x2st"],
                                           env["mq2k"])

            @blk.scalar
            def _(s):
                w8(s, state["qk_done"])
                sMk.inc(s.dma_start(out=mk2_d, in_=mk2[:, :]), 16)

            @blk.gpsimd
            def _(g):
                for p in range(2):
                    w8(g, sKw[p].mark())
                    w8(g, sVw[p].mark())
                w8(g, sMk.mark())
                sCC.inc(g.collective_compute(
                    "AllGather", ALU.bypass, replica_groups=groups,
                    ins=[k_loc], outs=[k_full]))
                sCC.inc(g.collective_compute(
                    "AllGather", ALU.bypass, replica_groups=groups,
                    ins=[v_loc], outs=[v_full]))
                sCC.inc(g.collective_compute(
                    "AllReduce", ALU.max, replica_groups=groups,
                    ins=[mk2_d], outs=[mk2r_d]))
                g.wait_ge(sCC.h, sCC.n)
                for kh in range(NKH):
                    for r in range(2):
                        sKV.inc(g.dma_start(
                            out=K_sb[:, kh * 2 * T + r * T:
                                     kh * 2 * T + (r + 1) * T],
                            in_=k_full[r * 1024 + kh * 128:
                                       r * 1024 + (kh + 1) * 128, :]), 16)
                for b4 in range(KC // 8):
                    sKV.inc(g.dma_start(
                        out=V_sb[:, b4 * 8192:(b4 + 1) * 8192].rearrange(
                            "p (kc d) -> p kc d", kc=8),
                        in_=v_full[b4 * 1024:(b4 + 1) * 1024, :].rearrange(
                            "(kc p) d -> p kc d", p=128)), 16)
                sKV.inc(g.dma_start(out=mk2[:, :], in_=mk2r_d), 16)
            kv_res = sKV.mark()

            @blk.vector
            def _(v):
                w8(v, kv_res)
                m = v.tensor_mul(mq2k[0:1, :], mk2[0:1, :], mq2[0:1, :])
                state["ub1"] = sFin.inc(m)

            @blk.scalar
            def _(s):
                w8(s, state["ub1"])
                s.activation(mq2l[0:1, :], mq2k[0:1, :], AF.Ln)
                m = s.activation(mq2k[0:1, :], mq2l[0:1, :], AF.Exp, scale=0.5)
                state["ub2"] = sFin.inc(m)

            @blk.vector
            def _(v):
                w8(v, state["ub2"])
                m = v.tensor_scalar(mq2k[0:1, :], mq2k[0:1, :], -INVSQ, -1.0,
                                    ALU.mult, ALU.add)
                state["ub3"] = sFin.inc(m)

            @blk.tensor
            def _(t):
                w8(t, state["ub3"])
                m = t.matmul(P[6][:, 0:32], ones1[:, :], mq2k[0:1, :],
                             start=True, stop=True)
                state["ub4"] = sLnM.inc(m)

            @blk.vector
            def _(v):
                w8(v, state["ub4"])
                m = v.tensor_copy(ubc[:, :], P[6][:, 0:32])
                state["ub"] = sFin.inc(m)

            base_qk = sQK.n
            base_exp = sExp.n
            base_pv = sPV.n
            base_aev = sAev.n
            base_aevw = sAevW.n
            base_x2w = [sX2w[0].n, sX2w[1].n]

            mQh = []

            @blk.gpsimd
            def _(g):
                for h in range(NQH):
                    extra = [kv_res]
                    if h >= 2:
                        extra.append((sPV, base_pv + KC * NBLK * (h - 1)))
                    mQh.append(qhs.load(g, q_d[h], extra_gates=extra)[1])

            mX1 = []

            @blk.sync
            def _(q):
                x1g = [sX1w[0].mark(), sX1w[1].mark()]
                for h in range(NQH):
                    for qg in range(NBLK):
                        a = h * NBLK + qg
                        extra = list(x1g) if a < 2 else []
                        if a >= 2:
                            extra.append((sAevW, base_aevw + a - 1))
                        mX1.append(x1s.load(
                            q, x1_d[h * 128:(h + 1) * 128,
                                    qg * TB:(qg + 1) * TB],
                            extra_gates=extra)[1])

            @blk.tensor
            def _(t):
                w8(t, state["ub"])
                for h in range(NQH):
                    kh = h // 4
                    qp = h % 2
                    w8(t, mQh[h])
                    for qg in range(NBLK):
                        a = h * NBLK + qg
                        ob = P[2 + (a % 2) * 2]
                        sb = P[3 + (a % 2) * 2]
                        K0 = KC * a

                        def qk(kc):
                            kk = K0 + kc
                            if kk >= 2:
                                t.wait_ge(sExp.h, base_exp + kk - 1)
                            m = t.matmul(
                                P[kc % 2][:, :],
                                K_sb[:, kh * 2 * T + kc * 128:
                                     kh * 2 * T + (kc + 1) * 128],
                                qhs.bufs[qp][:, qg * TB:(qg + 1) * TB],
                                start=True, stop=True)
                            sQK.inc(m)
                        qk(0)
                        qk(1)
                        for kc in range(KC):
                            t.wait_ge(sExp.h, base_exp + K0 + kc + 1)
                            if kc == 0 and a >= 2:
                                t.wait_ge(sAev.h, base_aev + a - 1)
                            t.matmul(ob[:, :],
                                     V_sb[:, kc * 1024 + kh * 128:
                                          kc * 1024 + (kh + 1) * 128],
                                     pb[kc % 2][:, :],
                                     start=(kc == 0), stop=(kc == KC - 1))
                            m = t.matmul(sb[:, :], onesm[:, 0:128],
                                         pb[kc % 2][:, :],
                                         start=(kc == 0), stop=(kc == KC - 1))
                            sPV.inc(m)
                            if kc + 2 < KC:
                                qk(kc + 2)
                    qhs.consumed(qp, sPV.mark())

            def _x2spill(s, a):
                p = a % 2
                s.wait_ge(sAevW.h, base_aevw + a + 1)
                h2, qg2 = a // NBLK, a % NBLK
                sX2w[p].inc(s.dma_start(
                    out=x2_d[h2 * 128:(h2 + 1) * 128,
                             qg2 * TB:(qg2 + 1) * TB],
                    in_=x2st[p][:, :]), 16)

            @blk.scalar
            def _(s):
                for h in range(NQH):
                    for qg in range(NBLK):
                        a = h * NBLK + qg
                        K0 = KC * a
                        for kc in range(KC):
                            kk = K0 + kc
                            s.wait_ge(sQK.h, base_qk + kk + 1)
                            if kk >= 2:
                                s.wait_ge(sPV.h, base_pv + kk - 1)
                            m = s.activation(pb[kc % 2][:, :], P[kc % 2][:, :],
                                             AF.Exp, bias=ubc[:, h:h + 1],
                                             scale=INVSQ)
                            sExp.inc(m)
                        if a >= 2:
                            _x2spill(s, a - 2)
                _x2spill(s, NQH * NBLK - 2)
                _x2spill(s, NQH * NBLK - 1)

            @blk.vector
            def _(v):
                for h in range(NQH):
                    for qg in range(NBLK):
                        a = h * NBLK + qg
                        p = a % 2
                        v.wait_ge(sPV.h, base_pv + KC * (a + 1))
                        w8(v, mX1[a])
                        if a >= 2:
                            v.wait_ge(sX2w[p].h,
                                      base_x2w[p] + 16 * ((a - 2) // 2 + 1))
                        v.reciprocal(rbuf[:, :], P[3 + p * 2][:, :])
                        m = v.tensor_mul(rbuf2[:, :], P[2 + p * 2][:, :],
                                         rbuf[:, :])
                        sAev.inc(m)
                        m = v.tensor_add(x2st[p][:, :], rbuf2[:, :],
                                         x1s.bufs[p][:, :])
                        mk = sAevW.inc(m)
                        x1s.consumed(p, mk)



        with ExitStack() as actx:
            env["u_sb"] = actx.enter_context(
                nc.sbuf_tensor("u_sb", [128, IC * TB], BF16))
            env["mid_sb"] = actx.enter_context(
                nc.sbuf_tensor("mid_sb", [128, HC * TB], BF16))
            env["ntmp"] = actx.enter_context(
                nc.sbuf_tensor("ntmp", [128, TB], F32))
            env["xstr"] = Stream(actx, nc, "xs", [128, 1024], F32)
            env["wbA"] = WStream(actx, nc, "wA", [128, IC * 128], BF16)
            env["wbB"] = WStream(actx, nc, "wB", [128, IC * 128], BF16)
            env["wbC"] = WStream(actx, nc, "wC", [128, 2048], BF16)
            env["ebuf"] = [actx.enter_context(
                nc.sbuf_tensor(f"eb{i}", [128, 1024], BF16)) for i in range(2)]
            env["fbuf"] = [actx.enter_context(
                nc.sbuf_tensor(f"fb{i}", [128, 1024], BF16)) for i in range(2)]
            env["gbuf"] = [actx.enter_context(
                nc.sbuf_tensor(f"gb{i}", [128, TB], F32)) for i in range(2)]
            env["vstb"] = actx.enter_context(
                nc.sbuf_tensor("vs0", [128, 2048], BF16))

            prelude()
            for B in range(NBLK):
                conv = B == 0
                emit_ln(B, x_in)
                state["gu_base_for_down"] = sEvGU.n
                emit_gu(B, conv)
                emit_down(B, conv, x_in, x1_d, sX1w, ride_ln2=True)
                emit_ln2_fin(B, x1_d)
                emit_qkv(B, conv)
            scrA = env["wbA"].wmarks()
            scrB = env["wbB"].wmarks()
            scrC = env["wbC"].wmarks()
            wA_sa, wB_sa, wC_sa = env["wbA"], env["wbB"], env["wbC"]

        barrier(pre={
            "scalar": [sQw[0].mark(), sQw[1].mark(), sKw[0].mark(),
                       sKw[1].mark(), sVw[0].mark(), sVw[1].mark(),
                       sX1w[0].mark(), sX1w[1].mark()],
            "sync": scrA + scrB + scrC,
        })

        with ExitStack() as bctx:
            env["K_sb"] = bctx.enter_context(
                nc.sbuf_tensor("K_sb", [128, NKH * 2 * T], BF16))
            env["V_sb"] = bctx.enter_context(
                nc.sbuf_tensor("V_sb", [128, KC * 1024], BF16))
            env["qhs"] = Stream(bctx, nc, "qh", [128, T], BF16)
            env["x1s"] = Stream(bctx, nc, "x1s", [128, TB], F32)
            env["pb"] = [bctx.enter_context(
                nc.sbuf_tensor(f"pb{i}", [128, TB], BF16)) for i in range(2)]
            env["rbuf"] = bctx.enter_context(
                nc.sbuf_tensor("rbuf", [128, TB], F32))
            env["rbuf2"] = bctx.enter_context(
                nc.sbuf_tensor("rbuf2", [128, TB], F32))
            env["x2st"] = [bctx.enter_context(
                nc.sbuf_tensor(f"x2st{i}", [128, TB], F32)) for i in range(2)]
            env["mq2k"] = bctx.enter_context(
                nc.sbuf_tensor("mq2k", [1, 32], F32))
            for i in range(8):
                pgate[i] = None
            emit_attention()

        barrier(pre={
            "scalar": [sX2w[0].mark(), sX2w[1].mark()],
        })

        with ExitStack() as cctx:
            env["u_sb"] = cctx.enter_context(
                nc.sbuf_tensor("u_sbC", [128, IC * TB], BF16))
            env["mid_sb"] = cctx.enter_context(
                nc.sbuf_tensor("mid_sbC", [128, HC * TB], BF16))
            env["ntmp"] = cctx.enter_context(
                nc.sbuf_tensor("ntmpC", [128, TB], F32))
            env["xstr"] = Stream(cctx, nc, "xsC", [128, 1024], F32)
            env["wbA"] = WStream(cctx, nc, "wAC", [128, IC * 128], BF16)
            env["wbB"] = WStream(cctx, nc, "wBC", [128, IC * 128], BF16)
            env["wbC"] = WStream(cctx, nc, "wCC", [128, 2048], BF16)
            env["ebuf"] = [cctx.enter_context(
                nc.sbuf_tensor(f"ebC{i}", [128, 1024], BF16)) for i in range(2)]
            env["fbuf"] = [cctx.enter_context(
                nc.sbuf_tensor(f"fbC{i}", [128, 1024], BF16)) for i in range(2)]
            env["gbuf"] = [cctx.enter_context(
                nc.sbuf_tensor(f"gbC{i}", [128, TB], F32)) for i in range(2)]
            for i in range(8):
                pgate[i] = None
            state["ebuf_gate"] = None
            for B in range(NBLK):
                emit_ln(B, x2_d)
                state["gu_base_for_down"] = sEvGU.n
                emit_gu(B, False, first_scr_gates=(scrA + scrB) if B == 0 else ())
                emit_down(B, False, x2_d, y_out, sYw, ride_ln2=False,
                          first_scr_gates=scrC if B == 0 else ())

            @blk.scalar
            def _(s):
                s.wait_ge(sYw[0].h, sYw[0].n)
                s.wait_ge(sYw[1].h, sYw[1].n)

            if debug:
                sDbg = Sem(ctx, nc, "sDbg")

                @blk.gpsimd
                def _(g):
                    g.wait_ge(sYw[0].h, sYw[0].n)
                    g.wait_ge(sYw[1].h, sYw[1].n)
                    for name, src_ap in (("x1", x1_d), ("x2", x2_d),
                                         ("kf", k_full), ("vf", v_full),
                                         ("q0", q_d[0])):
                        sDbg.inc(g.dma_start(out=dbg[name], in_=src_ap), 16)
                    sDbg.inc(g.dma_start(out=dbg["mq2"], in_=mq2[:, :]), 16)
                    sDbg.inc(g.dma_start(out=dbg["mk2"], in_=mk2[:, :]), 16)
                    sDbg.inc(g.dma_start(out=dbg["ubc"], in_=ubc[:, :]), 16)
                    g.wait_ge(sDbg.h, sDbg.n)

    return nc


def kernel(L0_attn_x, L0_attn_wq, L0_attn_bq, L0_attn_wk, L0_attn_bk,
           L0_attn_wv, L0_attn_bv, L0_ffn_w_gate, L0_ffn_w_up, L0_ffn_w_down,
           L0_ffn_scale_gate, L0_ffn_scale_up, L0_ffn_scale_down):
    bf16 = ml_dtypes.bfloat16
    x = np.asarray(L0_attn_x, dtype=np.float32)
    wq = np.asarray(L0_attn_wq, dtype=np.float32)
    bq = np.asarray(L0_attn_bq, dtype=np.float32)
    wk = np.asarray(L0_attn_wk, dtype=np.float32)
    bk = np.asarray(L0_attn_bk, dtype=np.float32)
    wv = np.asarray(L0_attn_wv, dtype=np.float32)
    bv = np.asarray(L0_attn_bv, dtype=np.float32)
    wg = np.asarray(L0_ffn_w_gate, dtype=np.int32)
    wu = np.asarray(L0_ffn_w_up, dtype=np.int32)
    wd = np.asarray(L0_ffn_w_down, dtype=np.int32)

    wg4 = np.ascontiguousarray(
        wg.T.reshape(32, 128, HC, 128).transpose(2, 1, 0, 3))
    wu4 = np.ascontiguousarray(
        wu.T.reshape(32, 128, HC, 128).transpose(2, 1, 0, 3))
    wd4 = np.ascontiguousarray(
        wd.T.reshape(HC, 128, NDG, 256).transpose(2, 0, 1, 3))
    wq4 = wq.reshape(32, 128, 32, 128).transpose(2, 1, 0, 3)
    wk4 = wk.reshape(32, 128, 8, 128).transpose(2, 1, 0, 3)
    wqk4 = np.ascontiguousarray(np.concatenate([wq4, wk4], axis=0))
    wvr = np.ascontiguousarray(wv.reshape(32, 128, 1024))
    bqk = np.ascontiguousarray(
        np.concatenate([bq.reshape(32, 128), bk.reshape(8, 128)], axis=0).T)
    bvb = np.ascontiguousarray(np.broadcast_to(bv, (128, 1024)))
    scg_a = np.ascontiguousarray(
        np.broadcast_to(np.float32(L0_ffn_scale_gate), (128, 1)))
    scu_a = np.ascontiguousarray(
        np.broadcast_to(np.float32(L0_ffn_scale_up), (128, 1)))
    scd_a = np.ascontiguousarray(
        np.broadcast_to(np.float32(L0_ffn_scale_down), (128, 1)))

    rmtm = np.zeros((128, 128), np.float32)
    rmtm[np.arange(64) + 64, np.arange(64)] = -1.0
    rmtm[np.arange(64), np.arange(64) + 64] = 1.0
    rmtm = rmtm.astype(bf16)
    onesc = np.ones((128, 129), bf16)
    ones1 = np.ones((1, 128), np.float32)
    inv = 1.0 / 10000.0 ** (np.arange(0, 128, 2, dtype=np.float64) / 128.0)

    in_maps = []
    for c in range(8):
        b, hh = c // 2, c % 2
        xc = np.ascontiguousarray(x[b].T[:, hh * T:(hh + 1) * T])
        pos = hh * T + np.arange(T, dtype=np.float64)
        ang = pos[:, None] * inv[None, :]
        cosr = np.cos(ang).T
        sinr = np.sin(ang).T
        in_maps.append(dict(
            x_fm=xc, wg4=wg4, wu4=wu4, wd4=wd4, wqk4=wqk4, wvr=wvr,
            bqk=bqk, bvb=bvb, scg=scg_a, scu=scu_a, scd=scd_a,
            cosd=np.ascontiguousarray(
                np.concatenate([cosr, cosr], 0)).astype(bf16),
            sind=np.ascontiguousarray(
                np.concatenate([sinr, sinr], 0)).astype(bf16),
            rmtd=rmtm, onesd=onesc, ones1d=ones1,
        ))

    nc = build_nc()
    res = run_bass_kernel_spmd(nc, in_maps, core_ids=list(range(8)))

    out = np.empty((4, 4096, 4096), np.float32)
    for c in range(8):
        b, hh = c // 2, c % 2
        out[b, hh * T:(hh + 1) * T, :] = res.results[c]["y_fm"].T
    return out


# version 03
# speedup vs baseline: 1.1378x; 1.1378x over previous
import numpy as np
import ml_dtypes
from contextlib import ExitStack

import concourse.bass as bass
import concourse.mybir as mybir
from concourse.bass_utils import run_bass_kernel_spmd

F32 = mybir.dt.float32
BF16 = mybir.dt.bfloat16
I32 = mybir.dt.int32
AF = mybir.ActivationFunctionType
ALU = mybir.AluOpType

T = 2048
TB = 512
NBLK = 4
KC = 32
IC = 32
HC = 86
NDG = 16
NQH = 32
NKH = 8
INVSQ = 0.08838834764831845
RECD = 1.0 / 4096.0
EPS = 1e-5
GROUPS = [[0, 1], [2, 3], [4, 5], [6, 7]]


class Sem:
    __slots__ = ("h", "n")

    def __init__(self, ctx, nc, name):
        self.h = ctx.enter_context(nc.semaphore(name))
        self.n = 0

    def inc(self, instr, k=1):
        instr.then_inc(self.h, k)
        self.n += k
        return (self, self.n)

    def mark(self):
        return (self, self.n)


def w8(eng, mark):
    if mark is not None:
        eng.wait_ge(mark[0].h, mark[1])


class Stream:

    def __init__(self, ctx, nc, name, shape, dtype):
        self.bufs = [ctx.enter_context(nc.sbuf_tensor(f"{name}{i}", shape, dtype))
                     for i in range(2)]
        self.sems = [Sem(ctx, nc, f"s_{name}{i}") for i in range(2)]
        self.gates = [[], []]
        self.marks = [None, None]
        self.idx = 0

    def load(self, q, in_ap, out_slicer=None, extra_gates=()):
        p = self.idx % 2
        for g in self.gates[p]:
            w8(q, g)
        for g in extra_gates:
            w8(q, g)
        self.gates[p] = []
        out = self.bufs[p][:, :] if out_slicer is None else out_slicer(self.bufs[p])
        self.marks[p] = self.sems[p].inc(q.dma_start(out=out, in_=in_ap), 16)
        self.idx += 1
        return p, self.marks[p]

    def ready(self, p):
        return self.marks[p]

    def consumed(self, p, mark):
        self.gates[p].append(mark)


class WStream(Stream):

    def __init__(self, ctx, nc, name, shape, dtype):
        super().__init__(ctx, nc, name, shape, dtype)
        self.wsems = [Sem(ctx, nc, f"s_{name}w{i}") for i in range(2)]

    def scratch_write(self, q, p, ld_mark, out_ap, src_slicer=None):
        w8(q, ld_mark)
        src = self.bufs[p][:, :] if src_slicer is None else src_slicer(self.bufs[p])
        self.wsems[p].inc(q.dma_start(out=out_ap, in_=src), 16)

    def wmarks(self):
        return [self.wsems[0].mark(), self.wsems[1].mark()]


def build_nc(groups=None, debug=False):
    assert NBLK == T // TB and KC == 2 * T // 128
    groups = groups or GROUPS
    nc = bass.Bass("TRN2", target_bir_lowering=False)
    dbg = {}
    if debug:
        dbg["x1"] = nc.dram_tensor("o_x1", [4096, T], F32, kind="ExternalOutput").ap()
        dbg["x2"] = nc.dram_tensor("o_x2", [4096, T], F32, kind="ExternalOutput").ap()
        dbg["kf"] = nc.dram_tensor("o_kf", [2048, T], BF16, kind="ExternalOutput").ap()
        dbg["vf"] = nc.dram_tensor("o_vf", [2 * T, 1024], BF16, kind="ExternalOutput").ap()
        dbg["q0"] = nc.dram_tensor("o_q0", [128, T], BF16, kind="ExternalOutput").ap()
        dbg["mq2"] = nc.dram_tensor("o_mq2", [1, 32], F32, kind="ExternalOutput").ap()
        dbg["mk2"] = nc.dram_tensor("o_mk2", [1, 32], F32, kind="ExternalOutput").ap()
        dbg["ubc"] = nc.dram_tensor("o_ubc", [128, 32], F32, kind="ExternalOutput").ap()

    x_in = nc.dram_tensor("x_fm", [4096, T], F32, kind="ExternalInput").ap()
    wg4 = nc.dram_tensor("wg4", [HC, 128, IC, 128], I32, kind="ExternalInput").ap()
    wu4 = nc.dram_tensor("wu4", [HC, 128, IC, 128], I32, kind="ExternalInput").ap()
    wd4 = nc.dram_tensor("wd4", [NDG, HC, 128, 256], I32, kind="ExternalInput").ap()
    wqk4 = nc.dram_tensor("wqk4", [40, 128, IC, 128], F32, kind="ExternalInput").ap()
    wvr = nc.dram_tensor("wvr", [IC, 128, 1024], F32, kind="ExternalInput").ap()
    bqk = nc.dram_tensor("bqk", [128, 40], F32, kind="ExternalInput").ap()
    bvb = nc.dram_tensor("bvb", [128, 1024], F32, kind="ExternalInput").ap()
    scg = nc.dram_tensor("scg", [128, 1], F32, kind="ExternalInput").ap()
    scu = nc.dram_tensor("scu", [128, 1], F32, kind="ExternalInput").ap()
    scd = nc.dram_tensor("scd", [128, 1], F32, kind="ExternalInput").ap()
    cosd = nc.dram_tensor("cosd", [128, T], BF16, kind="ExternalInput").ap()
    sind = nc.dram_tensor("sind", [128, T], BF16, kind="ExternalInput").ap()
    rmtd = nc.dram_tensor("rmtd", [128, 128], BF16, kind="ExternalInput").ap()
    onesd = nc.dram_tensor("onesd", [128, 129], BF16, kind="ExternalInput").ap()
    ones1d = nc.dram_tensor("ones1d", [1, 128], F32, kind="ExternalInput").ap()

    y_out = nc.dram_tensor("y_fm", [4096, T], F32, kind="ExternalOutput").ap()

    wg16 = nc.dram_tensor("wg16", [HC, 128, IC * 128], BF16, kind="Internal").ap()
    wu16 = nc.dram_tensor("wu16", [HC, 128, IC * 128], BF16, kind="Internal").ap()
    wd16 = nc.dram_tensor("wd16", [NDG, HC, 128, 256], BF16, kind="Internal").ap()
    wqk16 = nc.dram_tensor("wqk16", [40, 128, IC * 128], BF16, kind="Internal").ap()
    wv16 = nc.dram_tensor("wv16", [IC, 128, 1024], BF16, kind="Internal").ap()
    x1_d = nc.dram_tensor("x1_d", [4096, T], F32, kind="Internal").ap()
    x2_d = nc.dram_tensor("x2_d", [4096, T], F32, kind="Internal").ap()
    q_d = nc.dram_tensor("q_d", [NQH, 128, T], BF16, kind="Internal").ap()
    k_loc = nc.dram_tensor("k_loc", [1024, T], BF16, kind="Internal").ap()
    v_loc = nc.dram_tensor("v_loc", [T, 1024], BF16, kind="Internal").ap()
    k_full = nc.dram_tensor("k_full", [2048, T], BF16, kind="Internal").ap()
    v_full = nc.dram_tensor("v_full", [2 * T, 1024], BF16, kind="Internal").ap()
    mk2_d = nc.dram_tensor("mk2_d", [1, 32], F32, kind="Internal").ap()
    mk2r_d = nc.dram_tensor("mk2r_d", [1, 32], F32, kind="Internal").ap()

    with ExitStack() as ctx:
        P = [ctx.enter_context(nc.psum_tensor(f"P{i}", [128, TB], F32))
             for i in range(8)]

        cosb = ctx.enter_context(nc.sbuf_tensor("cosb", [128, T], BF16))
        sinb = ctx.enter_context(nc.sbuf_tensor("sinb", [128, T], BF16))
        rmt = ctx.enter_context(nc.sbuf_tensor("rmt", [128, 128], BF16))
        onesm = ctx.enter_context(nc.sbuf_tensor("onesm", [128, 129], BF16))
        ones1 = ctx.enter_context(nc.sbuf_tensor("ones1", [1, 128], F32))
        bqk_sb = ctx.enter_context(nc.sbuf_tensor("bqk_sb", [128, 40], F32))
        bvb_sb = ctx.enter_context(nc.sbuf_tensor("bvb_sb", [128, 1024], F32))
        sc_g = ctx.enter_context(nc.sbuf_tensor("sc_g", [128, 1], F32))
        sc_u = ctx.enter_context(nc.sbuf_tensor("sc_u", [128, 1], F32))
        sc_d = ctx.enter_context(nc.sbuf_tensor("sc_d", [128, 1], F32))
        mq2 = ctx.enter_context(nc.sbuf_tensor("mq2", [1, 32], F32))
        mk2 = ctx.enter_context(nc.sbuf_tensor("mk2", [1, 32], F32))
        mq2l = ctx.enter_context(nc.sbuf_tensor("mq2l", [1, 32], F32))
        rtmp = ctx.enter_context(nc.sbuf_tensor("rtmp", [1, 1], F32))
        nbc = ctx.enter_context(nc.sbuf_tensor("nbc", [128, TB], BF16))
        rbc = ctx.enter_context(nc.sbuf_tensor("rbc", [128, TB], BF16))
        rowm = ctx.enter_context(nc.sbuf_tensor("rowm", [1, TB], F32))
        rowa = ctx.enter_context(nc.sbuf_tensor("rowa", [1, TB], F32))
        rowb = ctx.enter_context(nc.sbuf_tensor("rowb", [1, TB], F32))
        ubc = ctx.enter_context(nc.sbuf_tensor("ubc", [128, 32], F32))

        sLd = Sem(ctx, nc, "sLd")
        sMM = Sem(ctx, nc, "sMM")
        sSil = Sem(ctx, nc, "sSil")
        sEvGU = Sem(ctx, nc, "sEvGU")
        sMMd = Sem(ctx, nc, "sMMd")
        sEvD = Sem(ctx, nc, "sEvD")
        sLn2P = Sem(ctx, nc, "sLn2P")
        sLnP = Sem(ctx, nc, "sLnP")
        sLnM = Sem(ctx, nc, "sLnM")
        sFin = Sem(ctx, nc, "sFin")
        sNorm = Sem(ctx, nc, "sNorm")
        sQmm = Sem(ctx, nc, "sQmm")
        sQact = Sem(ctx, nc, "sQact")
        sRot = Sem(ctx, nc, "sRot")
        sQdve = Sem(ctx, nc, "sQdve")
        sQnm = Sem(ctx, nc, "sQnm")
        sQnr = Sem(ctx, nc, "sQnr")
        sVmm = Sem(ctx, nc, "sVmm")
        sVev = Sem(ctx, nc, "sVev")
        sQK = Sem(ctx, nc, "sQK")
        sExp = Sem(ctx, nc, "sExp")
        sPV = Sem(ctx, nc, "sPV")
        sAev = Sem(ctx, nc, "sAev")
        sAevW = Sem(ctx, nc, "sAevW")
        sCC = Sem(ctx, nc, "sCC")
        sKV = Sem(ctx, nc, "sKV")
        sBar = Sem(ctx, nc, "sBar")
        sMk = Sem(ctx, nc, "sMk")
        sX1w = [Sem(ctx, nc, f"sX1w{i}") for i in range(2)]
        sQw = [Sem(ctx, nc, f"sQw{i}") for i in range(2)]
        sKw = [Sem(ctx, nc, f"sKw{i}") for i in range(2)]
        sVw = [Sem(ctx, nc, f"sVw{i}") for i in range(2)]
        sX2w = [Sem(ctx, nc, f"sX2w{i}") for i in range(2)]
        sYw = [Sem(ctx, nc, f"sYw{i}") for i in range(2)]

        blk = ctx.enter_context(nc.Block())

        pgate = [None] * 8

        def bank_wait(eng, i):
            w8(eng, pgate[i])

        state = {}
        env = {}

        def prelude():
            @blk.sync
            def _(q):
                for dst, src in ((cosb, cosd), (sinb, sind), (rmt, rmtd),
                                 (onesm, onesd), (ones1, ones1d),
                                 (bqk_sb, bqk), (bvb_sb, bvb),
                                 (sc_g, scg), (sc_u, scu), (sc_d, scd)):
                    sLd.inc(q.dma_start(out=dst[:, :], in_=src), 16)
            ld_all = sLd.mark()

            @blk.vector
            def _(v):
                w8(v, ld_all)
                v.tensor_scalar_mul(sc_d[:, :], sc_d[:, :], 0.5)
                v.memset(mq2[:, :], 0.0)
                m = v.memset(mk2[:, :], 0.0)
                sFin.inc(m)

        def emit_ln(B, src_d):
            xstr, ebuf, fbuf = env["xstr"], env["ebuf"], env["fbuf"]
            u_sb, ntmp = env["u_sb"], env["ntmp"]
            c0, c1 = B * TB, (B + 1) * TB

            s0 = xstr.idx
            base_lnp = sLnP.n
            base_lnm = sLnM.n

            mX = []

            @blk.gpsimd
            def _(q):
                for icp in range(16):
                    ap = src_d[icp * 256:(icp + 1) * 256, c0:c1] \
                        .rearrange("(c p) t -> p c t", c=2)
                    extra = [(sLnP, base_lnp + icp - 1)] if icp >= 2 else []
                    mX.append(xstr.load(
                        q, ap, out_slicer=lambda b: b[:, :].rearrange(
                            "p (c t) -> p c t", c=2), extra_gates=extra)[1])
            eg = state.get("ebuf_gate") or []

            @blk.vector
            def _(v):
                for icp in range(16):
                    p = (s0 + icp) % 2
                    w8(v, mX[icp])
                    if icp >= 2:
                        v.wait_ge(sLnM.h, base_lnm + icp - 1)
                    else:
                        v.wait_ge(sLnM.h, base_lnm)
                        for g in eg:
                            w8(v, g)
                    v.tensor_copy(ebuf[p][:, :], xstr.bufs[p][:, :])
                    m = v.tensor_mul(fbuf[p][:, :], ebuf[p][:, :], ebuf[p][:, :])
                    mk = sLnP.inc(m)
                    xstr.consumed(p, mk)
            state["ebuf_gate"] = None

            @blk.tensor
            def _(t):
                for icp in range(16):
                    p = (s0 + icp) % 2
                    t.wait_ge(sLnP.h, base_lnp + icp + 1)
                    if icp == 0:
                        bank_wait(t, 6)
                        bank_wait(t, 7)
                    for half in range(2):
                        st = icp == 0 and half == 0
                        sp = icp == 15 and half == 1
                        t.matmul(P[6][0:1, :], onesm[:, 128:129],
                                 ebuf[p][:, half * TB:(half + 1) * TB],
                                 start=st, stop=sp)
                        m = t.matmul(P[7][0:1, :], onesm[:, 128:129],
                                     fbuf[p][:, half * TB:(half + 1) * TB],
                                     start=st, stop=sp)
                    sLnM.inc(m)
            stats_done = sLnM.mark()

            @blk.vector
            def _(v):
                w8(v, stats_done)
                v.tensor_scalar_mul(rowm[:, :], P[6][0:1, :], RECD)
                v.tensor_mul(rowa[:, :], rowm[:, :], rowm[:, :])
                v.scalar_tensor_tensor(rowb[:, :], P[7][0:1, :], RECD,
                                       rowa[:, :], ALU.mult, ALU.subtract)
                v.tensor_scalar_add(rowb[:, :], rowb[:, :], EPS)
                m = v.tensor_scalar_mul(rowm[:, :], rowm[:, :], -1.0)
                state["fin_var"] = sFin.inc(m)
            pgate[6] = state["fin_var"]
            pgate[7] = state["fin_var"]

            @blk.scalar
            def _(s):
                w8(s, state["fin_var"])
                s.activation(rowa[:, :], rowb[:, :], AF.Ln)
                m = s.activation(rowb[:, :], rowa[:, :], AF.Exp, scale=-0.5)
                state["fin_rstd"] = sFin.inc(m)

            @blk.tensor
            def _(t):
                w8(t, state["fin_rstd"])
                bank_wait(t, 2)
                bank_wait(t, 3)
                t.matmul(P[2][:, :], ones1[:, :], rowm[:, :], start=True, stop=True)
                m = t.matmul(P[3][:, :], ones1[:, :], rowb[:, :], start=True, stop=True)
                state["fin_bc"] = sLnM.inc(m)

            @blk.vector
            def _(v):
                w8(v, state["fin_bc"])
                v.tensor_copy(nbc[:, :], P[2][:, :])
                m = v.tensor_copy(rbc[:, :], P[3][:, :])
                state["fin_ev"] = sFin.inc(m)
            pgate[2] = state["fin_ev"]
            pgate[3] = state["fin_ev"]

            n0 = xstr.idx
            base_nrm = sNorm.n

            mN = []

            @blk.gpsimd
            def _(q):
                for icp in range(16):
                    ap = src_d[icp * 256:(icp + 1) * 256, c0:c1] \
                        .rearrange("(c p) t -> p c t", c=2)
                    extra = [(sNorm, base_nrm + icp - 1)] if icp >= 2 else []
                    mN.append(xstr.load(q, ap,
                              out_slicer=lambda b: b[:, :].rearrange(
                                  "p (c t) -> p c t", c=2),
                              extra_gates=extra)[1])

            base_norm_c = sNorm.n

            @blk.vector
            def _(v):
                for icn in range(16):
                    p = (n0 + icn) % 2
                    w8(v, mN[icn])
                    for half in range(2):
                        ic = icn * 2 + half
                        v.tensor_add(ntmp[:, :],
                                     xstr.bufs[p][:, half * TB:(half + 1) * TB],
                                     nbc[:, :])
                        m = v.tensor_mul(u_sb[:, ic * TB:(ic + 1) * TB],
                                         ntmp[:, :], rbc[:, :])
                    mk = sNorm.inc(m)
                    xstr.consumed(p, mk)
            state["norm_done"] = sNorm.mark()
            state["norm_base"] = base_norm_c

        def emit_gu(B, conv, first_scr_gates=()):
            xstr, gbuf = env["xstr"], env["gbuf"]
            u_sb, mid_sb = env["u_sb"], env["mid_sb"]
            wbA, wbB = env["wbA"], env["wbB"]
            base_mm = sMM.n
            base_sil = sSil.n
            base_ev = sEvGU.n
            l0 = wbA.idx
            baseWA = [wbA.wsems[0].n, wbA.wsems[1].n] if conv else None
            baseWB = [wbB.wsems[0].n, wbB.wsems[1].n] if conv else None
            mA, mB = [], []

            def loads(q):
                for g in range(HC):
                    extra = list(first_scr_gates) if g == 0 else []
                    if g >= 2:
                        extra.append((sMM, base_mm + g - 1))
                        if conv:
                            p = (l0 + g) % 2
                            nw = 16 * ((g - 2) // 2 + 1)
                            extra.append((wbA.wsems[p], baseWA[p] + nw))
                            extra.append((wbB.wsems[p], baseWB[p] + nw))
                    if conv:
                        mA.append(wbA.load(
                            q, wg4[g].rearrange("p ic h -> p (ic h)"),
                            extra_gates=extra)[1])
                        mB.append(wbB.load(
                            q, wu4[g].rearrange("p ic h -> p (ic h)"),
                            extra_gates=extra)[1])
                    else:
                        mA.append(wbA.load(q, wg16[g], extra_gates=extra)[1])
                        mB.append(wbB.load(q, wu16[g], extra_gates=extra)[1])
            getattr(blk, "gpsimd" if conv else "sync")(loads)

            if conv:
                @blk.sync
                def _(q):
                    for g in range(HC):
                        wbA.scratch_write(q, (l0 + g) % 2, mA[g], wg16[g])
                        wbB.scratch_write(q, (l0 + g) % 2, mB[g], wu16[g])

            nbase = state["norm_base"]

            @blk.tensor
            def _(t):
                for g in range(HC):
                    p = (l0 + g) % 2
                    w8(t, mA[g])
                    w8(t, mB[g])
                    if g >= 2:
                        t.wait_ge(sSil.h, base_sil + g - 1)
                        t.wait_ge(sEvGU.h, base_ev + g - 1)
                    elif g == 0:
                        for i in range(4):
                            bank_wait(t, i)
                    for ic in range(IC):
                        st = ic == 0
                        sp = ic == IC - 1
                        if g == 0:
                            t.wait_ge(sNorm.h, nbase + ic // 2 + 1)
                        t.matmul(P[p][:, :],
                                 wbA.bufs[p][:, ic * 128:(ic + 1) * 128],
                                 u_sb[:, ic * TB:(ic + 1) * TB], start=st, stop=sp)
                        m = t.matmul(P[2 + p][:, :],
                                     wbB.bufs[p][:, ic * 128:(ic + 1) * 128],
                                     u_sb[:, ic * TB:(ic + 1) * TB], start=st, stop=sp)
                    mk = sMM.inc(m)
                    wbA.consumed(p, mk)
                    wbB.consumed(p, mk)

            @blk.scalar
            def _(s):
                for g in range(HC):
                    p = (l0 + g) % 2
                    s.wait_ge(sMM.h, base_mm + g + 1)
                    if g >= 2:
                        s.wait_ge(sEvGU.h, base_ev + g - 1)
                    m = s.activation(gbuf[p][:, 0:TB], P[p][:, :], AF.Silu,
                                     scale=sc_g[:, 0:1])
                    sSil.inc(m)

            @blk.vector
            def _(v):
                for g in range(HC):
                    p = (l0 + g) % 2
                    v.wait_ge(sSil.h, base_sil + g + 1)
                    m = v.scalar_tensor_tensor(
                        mid_sb[:, g * TB:(g + 1) * TB], P[2 + p][:, :],
                        sc_u[:, 0:1], gbuf[p][:, 0:TB], ALU.mult, ALU.mult)
                    sEvGU.inc(m)
            state["gu_done"] = sEvGU.mark()
            for i in range(4):
                pgate[i] = state["gu_done"]

        def emit_down(B, conv, x_src, out_d, out_sems, ride_ln2,
                      first_scr_gates=()):
            xstr, ebuf, fbuf = env["xstr"], env["ebuf"], env["fbuf"]
            mid_sb, wbC = env["mid_sb"], env["wbC"]
            c0, c1 = B * TB, (B + 1) * TB
            runs = []
            for dg in range(NDG):
                for h0 in range(0, HC, 8):
                    runs.append((dg, h0, min(h0 + 8, HC) - h0))
            rpd = (HC + 7) // 8

            base_mmd = sMMd.n
            base_evd = sEvD.n
            base_l2p = sLn2P.n
            base_l2m = sLnM.n
            gu_base = state["gu_base_for_down"]
            l0 = wbC.idx
            x0 = xstr.idx
            baseWC = [wbC.wsems[0].n, wbC.wsems[1].n] if conv else None
            baseOut = [out_sems[0].n, out_sems[1].n]
            mC = []

            def wload(q, R):
                dg, h0, n = runs[R]
                extra = list(first_scr_gates) if R == 0 else []
                if R >= 2:
                    extra.append((sMMd, base_mmd + R - 1))
                    if conv:
                        p = (l0 + R) % 2
                        extra.append((wbC.wsems[p],
                                      baseWC[p] + 16 * ((R - 2) // 2 + 1)))
                src = (wd4 if conv else wd16)[dg, h0:h0 + n] \
                    .rearrange("hc p d -> p hc d")
                mC.append(wbC.load(
                    q, src,
                    out_slicer=lambda b, n=n: b[:, 0:n * 256].rearrange(
                        "p (hc d) -> p hc d", hc=n),
                    extra_gates=extra)[1])

            mXd = []

            def xload(q, dg):
                ap = x_src[dg * 256:(dg + 1) * 256, c0:c1] \
                    .rearrange("(c p) t -> p c t", c=2)
                extra = []
                if dg >= 2:
                    extra.append((sEvD, base_evd + dg - 1))
                    if ride_ln2:
                        extra.append((sLn2P, base_l2p + dg - 1))
                    p2 = dg % 2
                    extra.append((out_sems[p2],
                                  baseOut[p2] + 16 * ((dg - 2) // 2 + 1)))
                mXd.append(xstr.load(q, ap,
                           out_slicer=lambda b: b[:, :].rearrange(
                               "p (c t) -> p c t", c=2),
                           extra_gates=extra)[1])

            if conv:
                @blk.gpsimd
                def _(q):
                    for R in range(len(runs)):
                        wload(q, R)
                        if R % rpd == 1:
                            xload(q, R // rpd)
            else:
                @blk.sync
                def _(q):
                    for R in range(len(runs)):
                        wload(q, R)

                @blk.gpsimd
                def _(q):
                    for dg in range(NDG):
                        xload(q, dg)

            if conv:
                @blk.sync
                def _(q):
                    for R, (dg, h0, n) in enumerate(runs):
                        wbC.scratch_write(
                            q, (l0 + R) % 2, mC[R],
                            wd16[dg, h0:h0 + n].rearrange("hc p d -> p hc d"),
                            src_slicer=lambda b, n=n: b[:, 0:n * 256].rearrange(
                                "p (hc d) -> p hc d", hc=n))

            def _stats_mm(t, dg):
                p = (x0 + dg) % 2
                t.wait_ge(sLn2P.h, base_l2p + dg + 1)
                if dg == 0:
                    bank_wait(t, 0)
                    bank_wait(t, 1)
                for half in range(2):
                    st = dg == 0 and half == 0
                    sp = dg == NDG - 1 and half == 1
                    t.matmul(P[0][0:1, :], onesm[:, 128:129],
                             ebuf[p][:, half * TB:(half + 1) * TB],
                             start=st, stop=sp)
                    m = t.matmul(P[1][0:1, :], onesm[:, 128:129],
                                 fbuf[p][:, half * TB:(half + 1) * TB],
                                 start=st, stop=sp)
                sLnM.inc(m)

            @blk.tensor
            def _(t):
                for R, (dg, h0, n) in enumerate(runs):
                    p = (l0 + R) % 2
                    qq = dg % 2
                    w8(t, mC[R])
                    if h0 == 0:
                        if ride_ln2 and dg >= 2:
                            _stats_mm(t, dg - 2)
                        if dg >= 2:
                            t.wait_ge(sEvD.h, base_evd + dg - 1)
                        else:
                            bank_wait(t, 4 + qq * 2)
                            bank_wait(t, 5 + qq * 2)
                    t.wait_ge(sEvGU.h, gu_base + h0 + n)
                    for j in range(n):
                        hc = h0 + j
                        for dc in range(2):
                            m = t.matmul(
                                P[4 + qq * 2 + dc][:, :],
                                wbC.bufs[p][:, j * 256 + dc * 128:
                                            j * 256 + (dc + 1) * 128],
                                mid_sb[:, hc * TB:(hc + 1) * TB],
                                start=(hc == 0), stop=(hc == HC - 1))
                    mk = sMMd.inc(m)
                    wbC.consumed(p, mk)
                if ride_ln2:
                    _stats_mm(t, NDG - 2)
                    _stats_mm(t, NDG - 1)

            @blk.vector
            def _(v):
                for dg in range(NDG):
                    qq = dg % 2
                    p = (x0 + dg) % 2
                    v.wait_ge(sMMd.h, base_mmd + (dg + 1) * rpd)
                    w8(v, mXd[dg])
                    for dc in range(2):
                        m = v.scalar_tensor_tensor(
                            xstr.bufs[p][:, dc * TB:(dc + 1) * TB],
                            P[4 + qq * 2 + dc][:, :], sc_d[:, 0:1],
                            xstr.bufs[p][:, dc * TB:(dc + 1) * TB],
                            ALU.mult, ALU.add)
                    sEvD.inc(m)
                    if ride_ln2:
                        if dg >= 2:
                            v.wait_ge(sLnM.h, base_l2m + dg - 1)
                        else:
                            v.wait_ge(sLnM.h, base_l2m)
                        v.tensor_copy(ebuf[p][:, :], xstr.bufs[p][:, :])
                        m2 = v.tensor_mul(fbuf[p][:, :], ebuf[p][:, :],
                                          ebuf[p][:, :])
                        sLn2P.inc(m2)
            state["down_ev"] = sEvD.mark()
            for i in range(4, 8):
                pgate[i] = state["down_ev"]

            if ride_ln2:
                state["ln2_stats"] = sLnM.mark()

            @blk.scalar
            def _(s):
                for dg in range(NDG):
                    p = (x0 + dg) % 2
                    s.wait_ge(sEvD.h, base_evd + dg + 1)
                    out_ap = out_d[dg * 256:(dg + 1) * 256, c0:c1] \
                        .rearrange("(c p) t -> p c t", c=2)
                    m = out_sems[dg % 2].inc(
                        s.dma_start(out=out_ap,
                                    in_=xstr.bufs[p][:, :].rearrange(
                                        "p (c t) -> p c t", c=2)), 16)
                    xstr.consumed(p, m)
                    if ride_ln2:
                        xstr.consumed(p, (sLn2P, base_l2p + dg + 1))

        def emit_ln2_fin(B, src_d):
            xstr, u_sb, ntmp = env["xstr"], env["u_sb"], env["ntmp"]
            c0, c1 = B * TB, (B + 1) * TB

            @blk.vector
            def _(v):
                w8(v, state["ln2_stats"])
                v.tensor_scalar_mul(rowm[:, :], P[0][0:1, :], RECD)
                v.tensor_mul(rowa[:, :], rowm[:, :], rowm[:, :])
                v.scalar_tensor_tensor(rowb[:, :], P[1][0:1, :], RECD,
                                       rowa[:, :], ALU.mult, ALU.subtract)
                v.tensor_scalar_add(rowb[:, :], rowb[:, :], EPS)
                m = v.tensor_scalar_mul(rowm[:, :], rowm[:, :], -1.0)
                state["fin2_var"] = sFin.inc(m)
            pgate[0] = state["fin2_var"]
            pgate[1] = state["fin2_var"]

            @blk.scalar
            def _(s):
                w8(s, state["fin2_var"])
                s.activation(rowa[:, :], rowb[:, :], AF.Ln)
                m = s.activation(rowb[:, :], rowa[:, :], AF.Exp, scale=-0.5)
                state["fin2_rstd"] = sFin.inc(m)

            @blk.tensor
            def _(t):
                w8(t, state["fin2_rstd"])
                bank_wait(t, 2)
                bank_wait(t, 3)
                t.matmul(P[2][:, :], ones1[:, :], rowm[:, :], start=True, stop=True)
                m = t.matmul(P[3][:, :], ones1[:, :], rowb[:, :], start=True, stop=True)
                state["fin2_bc"] = sLnM.inc(m)

            @blk.vector
            def _(v):
                w8(v, state["fin2_bc"])
                v.tensor_copy(nbc[:, :], P[2][:, :])
                m = v.tensor_copy(rbc[:, :], P[3][:, :])
                state["fin2_ev"] = sFin.inc(m)
            pgate[2] = state["fin2_ev"]
            pgate[3] = state["fin2_ev"]

            n0 = xstr.idx
            base_nrm2 = sNorm.n
            x1g = [sX1w[0].mark(), sX1w[1].mark()]

            mN2 = []

            @blk.gpsimd
            def _(q):
                for icp in range(16):
                    ap = src_d[icp * 256:(icp + 1) * 256, c0:c1] \
                        .rearrange("(c p) t -> p c t", c=2)
                    extra = list(x1g) if icp < 2 else \
                        [(sNorm, base_nrm2 + icp - 1)]
                    mN2.append(xstr.load(q, ap,
                               out_slicer=lambda b: b[:, :].rearrange(
                                   "p (c t) -> p c t", c=2),
                               extra_gates=extra)[1])

            base_norm_c2 = sNorm.n

            @blk.vector
            def _(v):
                for icn in range(16):
                    p = (n0 + icn) % 2
                    w8(v, mN2[icn])
                    for half in range(2):
                        ic = icn * 2 + half
                        v.tensor_add(ntmp[:, :],
                                     xstr.bufs[p][:, half * TB:(half + 1) * TB],
                                     nbc[:, :])
                        m = v.tensor_mul(u_sb[:, ic * TB:(ic + 1) * TB],
                                         ntmp[:, :], rbc[:, :])
                    mk = sNorm.inc(m)
                    xstr.consumed(p, mk)
            state["norm_done"] = sNorm.mark()
            state["norm_base"] = base_norm_c2

        def emit_qkv(B, conv):
            ebuf, fbuf, gbuf = env["ebuf"], env["fbuf"], env["gbuf"]
            ntmp = env["ntmp"]
            u_sb, wbA, wbC, vstb = env["u_sb"], env["wbA"], env["wbC"], env["vstb"]
            c0, c1 = B * TB, (B + 1) * TB
            base_qmm = sQmm.n
            base_qact = sQact.n
            base_rot = sRot.n
            base_qdve = sQdve.n
            base_qnm = sQnm.n
            base_qnr = sQnr.n
            base_qw = [sQw[0].n, sQw[1].n]
            base_kw = [sKw[0].n, sKw[1].n]
            l0 = wbA.idx
            baseWQ = [wbA.wsems[0].n, wbA.wsems[1].n] if conv else None
            mQ = []

            def wloads(q):
                for oc in range(40):
                    extra = []
                    if oc >= 2:
                        extra = [(sQmm, base_qmm + oc - 1)]
                        if conv:
                            p = (l0 + oc) % 2
                            extra.append((wbA.wsems[p],
                                          baseWQ[p] + 16 * ((oc - 2) // 2 + 1)))
                    if conv:
                        mQ.append(wbA.load(
                            q, wqk4[oc].rearrange("p ic h -> p (ic h)"),
                            extra_gates=extra)[1])
                    else:
                        mQ.append(wbA.load(q, wqk16[oc],
                                           extra_gates=extra)[1])
            getattr(blk, "gpsimd" if conv else "sync")(wloads)

            if conv:
                @blk.sync
                def _(q):
                    for oc in range(40):
                        wbA.scratch_write(q, (l0 + oc) % 2, mQ[oc], wqk16[oc])

            def _rot(t, oc):
                t.wait_ge(sQact.h, base_qact + oc + 1)
                if oc >= 2:
                    t.wait_ge(sQdve.h, base_qdve + oc - 1)
                else:
                    bank_wait(t, 4 + oc % 2)
                m = t.matmul(P[4 + oc % 2][:, :], rmt[:, :],
                             ebuf[oc % 2][:, 0:TB], start=True, stop=True)
                sRot.inc(m)

            def _qn(t, oc):
                t.wait_ge(sQdve.h, base_qdve + oc + 1)
                if oc >= 1:
                    t.wait_ge(sQnr.h, base_qnr + oc)
                else:
                    bank_wait(t, 6)
                m = t.matmul(P[6][0:1, :], onesm[:, 128:129],
                             fbuf[oc % 2][:, 0:TB], start=True, stop=True)
                sQnm.inc(m)

            nbase = state["norm_base"]

            @blk.tensor
            def _(t):
                for oc in range(40):
                    p = (l0 + oc) % 2
                    w8(t, mQ[oc])
                    if oc >= 2:
                        t.wait_ge(sQact.h, base_qact + oc - 1)
                    elif oc == 0:
                        for i in range(4):
                            bank_wait(t, i)
                    bank = P[oc % 2] if oc < 32 else P[2 + oc % 2]
                    for ic in range(IC):
                        if oc == 0:
                            t.wait_ge(sNorm.h, nbase + ic // 2 + 1)
                        m = t.matmul(bank[:, :],
                                     wbA.bufs[p][:, ic * 128:(ic + 1) * 128],
                                     u_sb[:, ic * TB:(ic + 1) * TB],
                                     start=(ic == 0), stop=(ic == IC - 1))
                    mk = sQmm.inc(m)
                    wbA.consumed(p, mk)
                    if oc >= 1:
                        _rot(t, oc - 1)
                    if oc >= 2:
                        _qn(t, oc - 2)
                _rot(t, 39)
                _qn(t, 38)
                _qn(t, 39)

            def _spill(s, oc):
                p = oc % 2
                s.wait_ge(sQdve.h, base_qdve + oc + 1)
                if oc < 32:
                    out_ap = q_d[oc, :, c0:c1]
                    sem = sQw[p]
                else:
                    out_ap = k_loc[(oc - 32) * 128:(oc - 31) * 128, c0:c1]
                    sem = sKw[p]
                sem.inc(s.dma_start(out=out_ap, in_=ebuf[p][:, TB:2 * TB]), 16)

            @blk.scalar
            def _(s):
                for oc in range(40):
                    s.wait_ge(sQmm.h, base_qmm + oc + 1)
                    if oc >= 2:
                        s.wait_ge(sRot.h, base_rot + oc - 1)
                        s.wait_ge(sQdve.h, base_qdve + oc - 1)
                    bank = P[oc % 2] if oc < 32 else P[2 + oc % 2]
                    m = s.activation(ebuf[oc % 2][:, 0:TB], bank[:, :],
                                     AF.Identity, bias=bqk_sb[:, oc:oc + 1])
                    sQact.inc(m)
                    if oc >= 2:
                        _spill(s, oc - 2)
                _spill(s, 38)
                _spill(s, 39)

            def _qnr(v, oc):
                v.wait_ge(sQnm.h, base_qnm + oc + 1)
                v.tensor_reduce(rtmp[:, :], P[6][0:1, :], mybir.AxisListType.X,
                                ALU.max)
                if oc < 32:
                    m = v.tensor_max(mq2[0:1, oc:oc + 1], mq2[0:1, oc:oc + 1],
                                     rtmp[:, :])
                else:
                    for j in range(4):
                        c = (oc - 32) * 4 + j
                        m = v.tensor_max(mk2[0:1, c:c + 1], mk2[0:1, c:c + 1],
                                         rtmp[:, :])
                sQnr.inc(m)

            @blk.vector
            def _(v):
                for oc in range(40):
                    p = oc % 2
                    v.wait_ge(sQact.h, base_qact + oc + 1)
                    v.wait_ge(sRot.h, base_rot + oc + 1)
                    if oc >= 2:
                        po = oc - 2
                        if po < 32:
                            v.wait_ge(sQw[po % 2].h,
                                      base_qw[po % 2] + 16 * (po // 2 + 1))
                        else:
                            v.wait_ge(sKw[po % 2].h,
                                      base_kw[po % 2] + 16 * ((po - 32) // 2 + 1))
                    v.tensor_mul(gbuf[p][:, 0:TB], ebuf[p][:, 0:TB],
                                 cosb[:, c0:c1])
                    v.tensor_mul(ntmp[:, :], P[4 + p][:, :],
                                 sinb[:, c0:c1])
                    v.tensor_add(ebuf[p][:, TB:2 * TB], gbuf[p][:, 0:TB],
                                 ntmp[:, :])
                    m = v.tensor_mul(fbuf[p][:, 0:TB], ebuf[p][:, TB:2 * TB],
                                     ebuf[p][:, TB:2 * TB])
                    sQdve.inc(m)
                    if oc >= 1:
                        _qnr(v, oc - 1)
                _qnr(v, 39)

            state["qk_done"] = sQnr.mark()

            base_vmm = sVmm.n
            base_vev = sVev.n
            vl0 = wbC.idx
            baseWV = [wbC.wsems[0].n, wbC.wsems[1].n] if conv else None
            mV = []

            def vloads(q):
                for i in range(64):
                    ic = i % IC
                    extra = []
                    if i >= 2:
                        extra = [(sVmm, base_vmm + i - 1)]
                        if conv and i - 2 < IC:
                            p = (vl0 + i) % 2
                            extra.append((wbC.wsems[p],
                                          baseWV[p] + 16 * ((i - 2) // 2 + 1)))
                    src = wvr[ic] if conv else wv16[ic]
                    mV.append(wbC.load(q, src, out_slicer=lambda b: b[:, 0:1024],
                                       extra_gates=extra)[1])
            getattr(blk, "gpsimd" if conv else "sync")(vloads)

            if conv:
                @blk.sync
                def _(q):
                    for ic in range(IC):
                        wbC.scratch_write(q, (vl0 + ic) % 2, mV[ic], wv16[ic],
                                          src_slicer=lambda b: b[:, 0:1024])

            @blk.tensor
            def _(t):
                for tp in range(2):
                    for ic in range(IC):
                        i = tp * IC + ic
                        p = (vl0 + i) % 2
                        w8(t, mV[i])
                        if i == 0:
                            t.wait_ge(sQdve.h, base_qdve + 40)
                            t.wait_ge(sQnr.h, base_qnr + 40)
                            bank_wait(t, 7)
                        if ic == 0 and tp == 1:
                            t.wait_ge(sVev.h, base_vev + 1)
                        for ttl in range(2):
                            for dvg in range(2):
                                m = t.matmul(
                                    P[4 + ttl * 2 + dvg][:, :],
                                    u_sb[:, ic * TB + (tp * 2 + ttl) * 128:
                                         ic * TB + (tp * 2 + ttl + 1) * 128],
                                    wbC.bufs[p][:, dvg * TB:(dvg + 1) * TB],
                                    start=(ic == 0), stop=(ic == IC - 1))
                        mk = sVmm.inc(m)
                        wbC.consumed(p, mk)

            base_vw = [sVw[0].n, sVw[1].n]

            @blk.vector
            def _(v):
                for tp in range(2):
                    v.wait_ge(sVmm.h, base_vmm + (tp + 1) * IC)
                    v.wait_ge(sVw[0].h, base_vw[0] + 64 * tp)
                    v.wait_ge(sVw[1].h, base_vw[1])
                    for ttl in range(2):
                        for dvg in range(2):
                            m = v.tensor_add(
                                vstb[:, (ttl * 2 + dvg) * TB:
                                     (ttl * 2 + dvg + 1) * TB],
                                P[4 + ttl * 2 + dvg][:, :],
                                bvb_sb[:, dvg * TB:(dvg + 1) * TB])
                    sVev.inc(m)
            state["v_done"] = sVev.mark()
            for i in range(4, 8):
                pgate[i] = state["v_done"]
            pgate[0] = sQact.mark()
            pgate[1] = sQact.mark()

            @blk.scalar
            def _(s):
                for tp in range(2):
                    s.wait_ge(sVev.h, base_vev + tp + 1)
                    for ttl in range(2):
                        ttg = tp * 2 + ttl
                        for dvg in range(2):
                            sVw[tp].inc(s.dma_start(
                                out=v_loc[c0 + ttg * 128: c0 + (ttg + 1) * 128,
                                          dvg * TB:(dvg + 1) * TB],
                                in_=vstb[:, (ttl * 2 + dvg) * TB:
                                         (ttl * 2 + dvg + 1) * TB]), 16)

            state["ebuf_gate"] = [sQdve.mark(), sRot.mark(), sQnm.mark(),
                                  sQw[0].mark(), sQw[1].mark(),
                                  sKw[0].mark(), sKw[1].mark()]

        def barrier(pre=None):
            pre = pre or {}
            for eng in ("sync", "gpsimd", "scalar", "vector", "tensor"):
                def f(e, eng=eng):
                    for g in pre.get(eng, []):
                        w8(e, g)
                    sBar.inc(e.nop())
                getattr(blk, eng)(f)
            target = sBar.n

            def waiter(e):
                e.wait_ge(sBar.h, target)
            for eng in ("sync", "gpsimd", "scalar", "vector", "tensor"):
                getattr(blk, eng)(waiter)

        def emit_attention():
            K_sb, V_sb = env["K_sb"], env["V_sb"]
            qhs, x1s = env["qhs"], env["x1s"]
            pb, rbuf, rbuf2, x2st, mq2k = (env["pb"], env["rbuf"],
                                           env["rbuf2"], env["x2st"],
                                           env["mq2k"])

            @blk.scalar
            def _(s):
                w8(s, state["qk_done"])
                sMk.inc(s.dma_start(out=mk2_d, in_=mk2[:, :]), 16)

            @blk.gpsimd
            def _(g):
                for p in range(2):
                    w8(g, sKw[p].mark())
                    w8(g, sVw[p].mark())
                w8(g, sMk.mark())
                sCC.inc(g.collective_compute(
                    "AllGather", ALU.bypass, replica_groups=groups,
                    ins=[k_loc], outs=[k_full]))
                sCC.inc(g.collective_compute(
                    "AllGather", ALU.bypass, replica_groups=groups,
                    ins=[v_loc], outs=[v_full]))
                sCC.inc(g.collective_compute(
                    "AllReduce", ALU.max, replica_groups=groups,
                    ins=[mk2_d], outs=[mk2r_d]))
                g.wait_ge(sCC.h, sCC.n)
                for kh in range(NKH):
                    for r in range(2):
                        sKV.inc(g.dma_start(
                            out=K_sb[:, kh * 2 * T + r * T:
                                     kh * 2 * T + (r + 1) * T],
                            in_=k_full[r * 1024 + kh * 128:
                                       r * 1024 + (kh + 1) * 128, :]), 16)
                for b4 in range(KC // 8):
                    sKV.inc(g.dma_start(
                        out=V_sb[:, b4 * 8192:(b4 + 1) * 8192].rearrange(
                            "p (kc d) -> p kc d", kc=8),
                        in_=v_full[b4 * 1024:(b4 + 1) * 1024, :].rearrange(
                            "(kc p) d -> p kc d", p=128)), 16)
                sKV.inc(g.dma_start(out=mk2[:, :], in_=mk2r_d), 16)
            kv_res = sKV.mark()

            @blk.vector
            def _(v):
                w8(v, kv_res)
                m = v.tensor_mul(mq2k[0:1, :], mk2[0:1, :], mq2[0:1, :])
                state["ub1"] = sFin.inc(m)

            @blk.scalar
            def _(s):
                w8(s, state["ub1"])
                s.activation(mq2l[0:1, :], mq2k[0:1, :], AF.Ln)
                m = s.activation(mq2k[0:1, :], mq2l[0:1, :], AF.Exp, scale=0.5)
                state["ub2"] = sFin.inc(m)

            @blk.vector
            def _(v):
                w8(v, state["ub2"])
                m = v.tensor_scalar(mq2k[0:1, :], mq2k[0:1, :], -INVSQ, -1.0,
                                    ALU.mult, ALU.add)
                state["ub3"] = sFin.inc(m)

            @blk.tensor
            def _(t):
                w8(t, state["ub3"])
                m = t.matmul(P[6][:, 0:32], ones1[:, :], mq2k[0:1, :],
                             start=True, stop=True)
                state["ub4"] = sLnM.inc(m)

            @blk.vector
            def _(v):
                w8(v, state["ub4"])
                m = v.tensor_copy(ubc[:, :], P[6][:, 0:32])
                state["ub"] = sFin.inc(m)

            base_qk = sQK.n
            base_exp = sExp.n
            base_pv = sPV.n
            base_aev = sAev.n
            base_aevw = sAevW.n
            base_x2w = [sX2w[0].n, sX2w[1].n]

            mQh = []

            qw_done = [sQw[0].mark(), sQw[1].mark()]

            @blk.gpsimd
            def _(g):
                for h in range(NQH):
                    extra = list(qw_done) if h < 2 else []
                    if h >= 2:
                        extra.append((sPV, base_pv + KC * NBLK * (h - 1)))
                    mQh.append(qhs.load(g, q_d[h], extra_gates=extra)[1])

            mX1 = []

            @blk.sync
            def _(q):
                x1g = [sX1w[0].mark(), sX1w[1].mark()]
                for h in range(NQH):
                    for qg in range(NBLK):
                        a = h * NBLK + qg
                        extra = list(x1g) if a < 2 else []
                        if a >= 2:
                            extra.append((sAevW, base_aevw + a - 1))
                        mX1.append(x1s.load(
                            q, x1_d[h * 128:(h + 1) * 128,
                                    qg * TB:(qg + 1) * TB],
                            extra_gates=extra)[1])

            SB = [P[0], P[1], P[6], P[7]]

            @blk.tensor
            def _(t):
                w8(t, state["ub"])
                for h in range(NQH):
                    kh = h // 4
                    qp = h % 2
                    w8(t, mQh[h])
                    for qg in range(NBLK):
                        a = h * NBLK + qg
                        ob = P[2 + (a % 2) * 2]
                        sb = P[3 + (a % 2) * 2]
                        K0 = KC * a

                        def qk(kc):
                            kk = K0 + kc
                            if kk >= 4:
                                t.wait_ge(sExp.h, base_exp + kk - 3)
                            m = t.matmul(
                                SB[kc % 4][:, :],
                                K_sb[:, kh * 2 * T + kc * 128:
                                     kh * 2 * T + (kc + 1) * 128],
                                qhs.bufs[qp][:, qg * TB:(qg + 1) * TB],
                                start=True, stop=True)
                            sQK.inc(m)
                        qk(0)
                        qk(1)
                        qk(2)
                        qk(3)
                        for kc in range(KC):
                            t.wait_ge(sExp.h, base_exp + K0 + kc + 1)
                            if kc == 0 and a >= 2:
                                t.wait_ge(sAev.h, base_aev + a - 1)
                            t.matmul(ob[:, :],
                                     V_sb[:, kc * 1024 + kh * 128:
                                          kc * 1024 + (kh + 1) * 128],
                                     pb[kc % 4][:, :],
                                     start=(kc == 0), stop=(kc == KC - 1))
                            m = t.matmul(sb[:, :], onesm[:, 0:128],
                                         pb[kc % 4][:, :],
                                         start=(kc == 0), stop=(kc == KC - 1))
                            sPV.inc(m)
                            if kc + 4 < KC:
                                qk(kc + 4)
                    qhs.consumed(qp, sPV.mark())

            def _x2spill(s, a):
                p = a % 2
                s.wait_ge(sAevW.h, base_aevw + a + 1)
                h2, qg2 = a // NBLK, a % NBLK
                sX2w[p].inc(s.dma_start(
                    out=x2_d[h2 * 128:(h2 + 1) * 128,
                             qg2 * TB:(qg2 + 1) * TB],
                    in_=x2st[p][:, :]), 16)

            @blk.scalar
            def _(s):
                for h in range(NQH):
                    for qg in range(NBLK):
                        a = h * NBLK + qg
                        K0 = KC * a
                        for kc in range(KC):
                            kk = K0 + kc
                            s.wait_ge(sQK.h, base_qk + kk + 1)
                            if kk >= 4:
                                s.wait_ge(sPV.h, base_pv + kk - 3)
                            m = s.activation(pb[kc % 4][:, :],
                                             SB[kc % 4][:, :],
                                             AF.Exp, bias=ubc[:, h:h + 1],
                                             scale=INVSQ)
                            sExp.inc(m)
                        if a >= 2:
                            _x2spill(s, a - 2)
                _x2spill(s, NQH * NBLK - 2)
                _x2spill(s, NQH * NBLK - 1)

            @blk.vector
            def _(v):
                for h in range(NQH):
                    for qg in range(NBLK):
                        a = h * NBLK + qg
                        p = a % 2
                        v.wait_ge(sPV.h, base_pv + KC * (a + 1))
                        w8(v, mX1[a])
                        if a >= 2:
                            v.wait_ge(sX2w[p].h,
                                      base_x2w[p] + 16 * ((a - 2) // 2 + 1))
                        v.reciprocal(rbuf[:, :], P[3 + p * 2][:, :])
                        m = v.tensor_mul(rbuf2[:, :], P[2 + p * 2][:, :],
                                         rbuf[:, :])
                        sAev.inc(m)
                        m = v.tensor_add(x2st[p][:, :], rbuf2[:, :],
                                         x1s.bufs[p][:, :])
                        mk = sAevW.inc(m)
                        x1s.consumed(p, mk)



        with ExitStack() as actx:
            env["u_sb"] = actx.enter_context(
                nc.sbuf_tensor("u_sb", [128, IC * TB], BF16))
            env["mid_sb"] = actx.enter_context(
                nc.sbuf_tensor("mid_sb", [128, HC * TB], BF16))
            env["ntmp"] = actx.enter_context(
                nc.sbuf_tensor("ntmp", [128, TB], F32))
            env["xstr"] = Stream(actx, nc, "xs", [128, 1024], F32)
            env["wbA"] = WStream(actx, nc, "wA", [128, IC * 128], BF16)
            env["wbB"] = WStream(actx, nc, "wB", [128, IC * 128], BF16)
            env["wbC"] = WStream(actx, nc, "wC", [128, 2048], BF16)
            env["ebuf"] = [actx.enter_context(
                nc.sbuf_tensor(f"eb{i}", [128, 1024], BF16)) for i in range(2)]
            env["fbuf"] = [actx.enter_context(
                nc.sbuf_tensor(f"fb{i}", [128, 1024], BF16)) for i in range(2)]
            env["gbuf"] = [actx.enter_context(
                nc.sbuf_tensor(f"gb{i}", [128, TB], F32)) for i in range(2)]
            env["vstb"] = actx.enter_context(
                nc.sbuf_tensor("vs0", [128, 2048], BF16))

            prelude()
            for B in range(NBLK):
                conv = B == 0
                emit_ln(B, x_in)
                state["gu_base_for_down"] = sEvGU.n
                emit_gu(B, conv)
                emit_down(B, conv, x_in, x1_d, sX1w, ride_ln2=True)
                emit_ln2_fin(B, x1_d)
                emit_qkv(B, conv)
            scrA = env["wbA"].wmarks()
            scrB = env["wbB"].wmarks()
            scrC = env["wbC"].wmarks()
            wA_sa, wB_sa, wC_sa = env["wbA"], env["wbB"], env["wbC"]

        barrier(pre={
            "scalar": [sQw[0].mark(), sQw[1].mark(), sKw[0].mark(),
                       sKw[1].mark(), sVw[0].mark(), sVw[1].mark(),
                       sX1w[0].mark(), sX1w[1].mark()],
            "sync": scrA + scrB + scrC,
        })

        with ExitStack() as bctx:
            env["K_sb"] = bctx.enter_context(
                nc.sbuf_tensor("K_sb", [128, NKH * 2 * T], BF16))
            env["V_sb"] = bctx.enter_context(
                nc.sbuf_tensor("V_sb", [128, KC * 1024], BF16))
            env["qhs"] = Stream(bctx, nc, "qh", [128, T], BF16)
            env["x1s"] = Stream(bctx, nc, "x1s", [128, TB], F32)
            env["pb"] = [bctx.enter_context(
                nc.sbuf_tensor(f"pb{i}", [128, TB], BF16)) for i in range(4)]
            env["rbuf"] = bctx.enter_context(
                nc.sbuf_tensor("rbuf", [128, TB], F32))
            env["rbuf2"] = bctx.enter_context(
                nc.sbuf_tensor("rbuf2", [128, TB], F32))
            env["x2st"] = [bctx.enter_context(
                nc.sbuf_tensor(f"x2st{i}", [128, TB], F32)) for i in range(2)]
            env["mq2k"] = bctx.enter_context(
                nc.sbuf_tensor("mq2k", [1, 32], F32))
            for i in range(8):
                pgate[i] = None
            emit_attention()

        barrier(pre={
            "scalar": [sX2w[0].mark(), sX2w[1].mark()],
        })

        with ExitStack() as cctx:
            env["u_sb"] = cctx.enter_context(
                nc.sbuf_tensor("u_sbC", [128, IC * TB], BF16))
            env["mid_sb"] = cctx.enter_context(
                nc.sbuf_tensor("mid_sbC", [128, HC * TB], BF16))
            env["ntmp"] = cctx.enter_context(
                nc.sbuf_tensor("ntmpC", [128, TB], F32))
            env["xstr"] = Stream(cctx, nc, "xsC", [128, 1024], F32)
            env["wbA"] = WStream(cctx, nc, "wAC", [128, IC * 128], BF16)
            env["wbB"] = WStream(cctx, nc, "wBC", [128, IC * 128], BF16)
            env["wbC"] = WStream(cctx, nc, "wCC", [128, 2048], BF16)
            env["ebuf"] = [cctx.enter_context(
                nc.sbuf_tensor(f"ebC{i}", [128, 1024], BF16)) for i in range(2)]
            env["fbuf"] = [cctx.enter_context(
                nc.sbuf_tensor(f"fbC{i}", [128, 1024], BF16)) for i in range(2)]
            env["gbuf"] = [cctx.enter_context(
                nc.sbuf_tensor(f"gbC{i}", [128, TB], F32)) for i in range(2)]
            for i in range(8):
                pgate[i] = None
            state["ebuf_gate"] = None
            for B in range(NBLK):
                emit_ln(B, x2_d)
                state["gu_base_for_down"] = sEvGU.n
                emit_gu(B, False, first_scr_gates=(scrA + scrB) if B == 0 else ())
                emit_down(B, False, x2_d, y_out, sYw, ride_ln2=False,
                          first_scr_gates=scrC if B == 0 else ())

            @blk.scalar
            def _(s):
                s.wait_ge(sYw[0].h, sYw[0].n)
                s.wait_ge(sYw[1].h, sYw[1].n)

            if debug:
                sDbg = Sem(ctx, nc, "sDbg")

                @blk.gpsimd
                def _(g):
                    g.wait_ge(sYw[0].h, sYw[0].n)
                    g.wait_ge(sYw[1].h, sYw[1].n)
                    for name, src_ap in (("x1", x1_d), ("x2", x2_d),
                                         ("kf", k_full), ("vf", v_full),
                                         ("q0", q_d[0])):
                        sDbg.inc(g.dma_start(out=dbg[name], in_=src_ap), 16)
                    sDbg.inc(g.dma_start(out=dbg["mq2"], in_=mq2[:, :]), 16)
                    sDbg.inc(g.dma_start(out=dbg["mk2"], in_=mk2[:, :]), 16)
                    sDbg.inc(g.dma_start(out=dbg["ubc"], in_=ubc[:, :]), 16)
                    g.wait_ge(sDbg.h, sDbg.n)

    return nc


def kernel(L0_attn_x, L0_attn_wq, L0_attn_bq, L0_attn_wk, L0_attn_bk,
           L0_attn_wv, L0_attn_bv, L0_ffn_w_gate, L0_ffn_w_up, L0_ffn_w_down,
           L0_ffn_scale_gate, L0_ffn_scale_up, L0_ffn_scale_down):
    bf16 = ml_dtypes.bfloat16
    x = np.asarray(L0_attn_x, dtype=np.float32)
    wq = np.asarray(L0_attn_wq, dtype=np.float32)
    bq = np.asarray(L0_attn_bq, dtype=np.float32)
    wk = np.asarray(L0_attn_wk, dtype=np.float32)
    bk = np.asarray(L0_attn_bk, dtype=np.float32)
    wv = np.asarray(L0_attn_wv, dtype=np.float32)
    bv = np.asarray(L0_attn_bv, dtype=np.float32)
    wg = np.asarray(L0_ffn_w_gate, dtype=np.int32)
    wu = np.asarray(L0_ffn_w_up, dtype=np.int32)
    wd = np.asarray(L0_ffn_w_down, dtype=np.int32)

    wg4 = np.ascontiguousarray(
        wg.T.reshape(32, 128, HC, 128).transpose(2, 1, 0, 3))
    wu4 = np.ascontiguousarray(
        wu.T.reshape(32, 128, HC, 128).transpose(2, 1, 0, 3))
    wd4 = np.ascontiguousarray(
        wd.T.reshape(HC, 128, NDG, 256).transpose(2, 0, 1, 3))
    wq4 = wq.reshape(32, 128, 32, 128).transpose(2, 1, 0, 3)
    wk4 = wk.reshape(32, 128, 8, 128).transpose(2, 1, 0, 3)
    wqk4 = np.ascontiguousarray(np.concatenate([wq4, wk4], axis=0))
    wvr = np.ascontiguousarray(wv.reshape(32, 128, 1024))
    bqk = np.ascontiguousarray(
        np.concatenate([bq.reshape(32, 128), bk.reshape(8, 128)], axis=0).T)
    bvb = np.ascontiguousarray(np.broadcast_to(bv, (128, 1024)))
    scg_a = np.ascontiguousarray(
        np.broadcast_to(np.float32(L0_ffn_scale_gate), (128, 1)))
    scu_a = np.ascontiguousarray(
        np.broadcast_to(np.float32(L0_ffn_scale_up), (128, 1)))
    scd_a = np.ascontiguousarray(
        np.broadcast_to(np.float32(L0_ffn_scale_down), (128, 1)))

    rmtm = np.zeros((128, 128), np.float32)
    rmtm[np.arange(64) + 64, np.arange(64)] = -1.0
    rmtm[np.arange(64), np.arange(64) + 64] = 1.0
    rmtm = rmtm.astype(bf16)
    onesc = np.ones((128, 129), bf16)
    ones1 = np.ones((1, 128), np.float32)
    inv = 1.0 / 10000.0 ** (np.arange(0, 128, 2, dtype=np.float64) / 128.0)

    in_maps = []
    for c in range(8):
        b, hh = c // 2, c % 2
        xc = np.ascontiguousarray(x[b].T[:, hh * T:(hh + 1) * T])
        pos = hh * T + np.arange(T, dtype=np.float64)
        ang = pos[:, None] * inv[None, :]
        cosr = np.cos(ang).T
        sinr = np.sin(ang).T
        in_maps.append(dict(
            x_fm=xc, wg4=wg4, wu4=wu4, wd4=wd4, wqk4=wqk4, wvr=wvr,
            bqk=bqk, bvb=bvb, scg=scg_a, scu=scu_a, scd=scd_a,
            cosd=np.ascontiguousarray(
                np.concatenate([cosr, cosr], 0)).astype(bf16),
            sind=np.ascontiguousarray(
                np.concatenate([sinr, sinr], 0)).astype(bf16),
            rmtd=rmtm, onesd=onesc, ones1d=ones1,
        ))

    nc = build_nc()
    res = run_bass_kernel_spmd(nc, in_maps, core_ids=list(range(8)))

    out = np.empty((4, 4096, 4096), np.float32)
    for c in range(8):
        b, hh = c // 2, c % 2
        out[b, hh * T:(hh + 1) * T, :] = res.results[c]["y_fm"].T
    return out
